# version 05
# baseline (speedup 1.0000x reference)
from contextlib import ExitStack

import numpy as np
import concourse.bass as bass
import concourse.mybir as mybir
from concourse.bass_utils import run_bass_kernel_spmd

F32 = mybir.dt.float32
BF16 = mybir.dt.bfloat16
AF = mybir.ActivationFunctionType
ALU = mybir.AluOpType

S = 2048
D = 2048
H = 16
KH = 4
HD = 128
W = 512
I = 8192
NTC = S // 128
NIC = D // 128
NOC = I // 512
NJC = I // 128
NDC = D // 128
SCL = 1.0 / np.sqrt(128.0)
C2P23 = 12582912.0
NEG = -1.0e30

ENGINES = ("sync", "gpsimd", "vector", "scalar", "tensor")
SEM_ENGINE = {"v": "vector", "sc": "scalar", "te": "tensor", "gp": "gpsimd"}


class Prog:

    def __init__(self, nc):
        self.nc = nc
        self.sem = {}
        self.counts = {}
        self.ops = {e: [] for e in ENGINES}
        self.last_wait = {e: {} for e in ENGINES}
        self.sems_used = set()

    def phase_barrier(self):
        toks = [(s, self.counts[s]) for s in self.sems_used]
        toks += [(e, self.counts[e]) for e in ("v", "sc", "te", "gp")]
        for e in ENGINES:
            self.op(e, None, waits=toks)
        self.sems_used = set()

    def make_sems(self, stack, names):
        for n in names:
            self.sem[n] = stack.enter_context(self.nc.semaphore(n))
            self.counts[n] = 0

    def op(self, eng, fn=None, waits=(), inc=None, incn=1):
        wl = {}
        for t in waits:
            if t is None:
                continue
            s, v = t
            if v <= 0 or SEM_ENGINE.get(s) == eng:
                continue
            if v > wl.get(s, -1):
                wl[s] = v
        wl2 = []
        for s, v in wl.items():
            if v > self.last_wait[eng].get(s, 0):
                self.last_wait[eng][s] = v
                wl2.append((s, v))
        tok = None
        if inc is not None:
            self.counts[inc] += incn
            tok = (inc, self.counts[inc])
        self.ops[eng].append((fn, wl2, inc, incn))
        return tok

    def dma(self, queue, sem, out, in_, waits=(), transpose=False,
            slow=False):
        semname = sem
        self.sems_used.add(sem)

        def fn(eng, out=out, in_=in_, transpose=transpose, slow=slow):
            if transpose:
                return eng.dma_start_transpose(out=out, in_=in_)
            return eng.dma_start(out=out, in_=in_,
                                 allow_slow_non_contiguous=slow)

        return self.op(queue, fn, waits=waits, inc=semname, incn=16)

    def tok(self, sem):
        return (sem, self.counts[sem])

    def emit(self, block):
        for eng_name in ENGINES:
            ops = self.ops[eng_name]
            if not ops:
                continue

            def make_body(ops=ops):
                def body(eng):
                    for fn, wl, inc, incn in ops:
                        for s, v in wl:
                            eng.wait_ge(self.sem[s], v)
                        if fn is not None:
                            ins = fn(eng)
                            if inc is not None:
                                ins.then_inc(self.sem[inc], incn)
                        elif inc is not None:
                            eng.nop().then_inc(self.sem[inc], incn)
                return body

            getattr(block, eng_name)(make_body())


def bcast_mid(ap2d, n):
    dims = [list(q) for q in ap2d.ap]
    return bass.AP(ap2d.tensor, ap2d.offset, [dims[0], [0, n]] + dims[1:])


def build_program():
    nc = bass.Bass("TRN2", target_bir_lowering=False, debug=False)
    dt = nc.dram_tensor
    x_in = dt("x", [S, D], F32, kind="ExternalInput").ap()
    xT_in = dt("xT", [D, S], F32, kind="ExternalInput").ap()
    lnw_in = dt("lnw", [NIC, 128], F32, kind="ExternalInput").ap()
    wq_in = dt("wq", [D, D], F32, kind="ExternalInput").ap()
    wk_in = dt("wk", [D, KH * HD], F32, kind="ExternalInput").ap()
    wv_in = dt("wv", [D, KH * HD], F32, kind="ExternalInput").ap()
    wo_in = dt("wo", [D, D], F32, kind="ExternalInput").ap()
    wgT_in = dt("wgT", [D, I], F32, kind="ExternalInput").ap()
    wuT_in = dt("wuT", [D, I], F32, kind="ExternalInput").ap()
    wdT_in = dt("wdT", [I, D], F32, kind="ExternalInput").ap()
    mask_in = dt("mask", [128, 640], F32, kind="ExternalInput").ap()
    ropeC_in = dt("ropeC", [S, 64], F32, kind="ExternalInput").ap()
    ropeS_in = dt("ropeS", [S, 64], F32, kind="ExternalInput").ap()
    ident_in = dt("ident", [128, 128], F32, kind="ExternalInput").ap()
    import os
    _K = "ExternalOutput" if os.environ.get("KDBG") else "Internal"
    xq_sp = dt("xq_sp", [NIC * 128, S], BF16, kind=_K).ap()
    q_sp = dt("q_sp", [S, D], BF16, kind=_K).ap()
    h_sp = dt("h_sp", [S, I], F32, kind=_K).ap()
    wdq_sp = dt("wdq_sp", [I, D], BF16, kind=_K).ap()
    part_sp = dt("part_sp", [D, S], F32, kind=_K).ap()
    dbg_sp = dt("dbg_sp", [128, 4 * D + 16], F32, kind=_K).ap()
    dbg_ao = dt("dbg_ao", [128, H * S], BF16, kind=_K).ap()
    dbg_kf = dt("dbg_kf", [128, KH * S], BF16, kind=_K).ap()
    dbg_vt = dt("dbg_vt", [128, NTC * KH * HD], BF16, kind=_K).ap()
    dbg_pt = dt("dbg_pt", [128, H * 5 * 128], BF16, kind=_K).ap()
    dbg_ps = dt("dbg_ps", [128, H * 5 * 128], BF16, kind=_K).ap()
    dbg_wob = dt("dbg_wob", [128, NDC * D], BF16, kind=_K).ap()
    dbg_psw = dt("dbg_psw", [128, 512], F32, kind=_K).ap()
    y_out = dt("y", [D, S], F32, kind="ExternalOutput").ap()

    gstack = ExitStack()
    p = Prog(nc)
    DMA_SEMS = ["c0", "f0a", "f0b", "xa", "xb", "na", "nb", "qca", "qcb",
                "xsa", "xsb", "wqa", "wqb", "rpa", "rpb", "qspa", "qspb",
                "wkv", "kfa", "kfb", "qta", "qtb", "qfa", "qfb", "ptx",
                "woa", "wob_", "xfa", "xfb_", "pta", "ptb", "xqr", "wgl",
                "wds", "hsa", "hsb", "hrl", "hqx", "wda", "wdb2", "pfa",
                "pfb2", "yoa", "yob", "dbq"]
    p.make_sems(gstack, ["v", "sc", "te", "gp"] + DMA_SEMS)
    sb = lambda name, shape, dtype=F32: gstack.enter_context(
        nc.sbuf_tensor(name, shape, dtype, side="left"))

    ident = sb("ident_s", [128, 128])
    mask = sb("mask_s", [128, 640])
    lnw_sb = sb("lnw_sb", [128, NIC])
    c_2p23 = sb("c_2p23", [128, 1])
    c_2p23p1 = sb("c_2p23p1", [128, 1])
    c_eps5 = sb("c_eps5", [128, 1])
    c_eps6 = sb("c_eps6", [128, 1])
    ones_col = sb("ones_col", [128, 1])
    ones_row = sb("ones_row", [1, 128])
    ws_vec = sb("ws_vec", [128, 4])
    rws_vec = sb("rws_vec", [128, 4])
    s_all = sb("s_all", [128, NTC])
    negA_g = sb("negA_g", [128, NTC])
    negA_u = sb("negA_u", [128, NTC])
    hs_all = sb("hs_all", [128, 4])
    dump = sb("dump", [128, 4096], BF16)
    dump_sc = sb("dump_sc", [128, D], BF16)
    wsacc = sb("wsacc", [128, 52])

    padv = sb("padv", [128, 1])
    pads = sb("pads", [128, 1])

    nc.const_aps.register(F32, C2P23, c_2p23[:, 0:1])

    def vpad():
        p.op("vector", lambda eng: eng.tensor_copy(padv[:], ones_col[:]))

    def spad():
        p.op("scalar", lambda eng: eng.copy(pads[:], ones_col[:]))

    for t, val in ((c_2p23, C2P23), (c_2p23p1, C2P23 + 1.0), (c_eps5, 1e-5),
                   (c_eps6, 1e-6), (ones_col, 1.0), (ones_row, 1.0)):
        p.op("vector", lambda eng, t=t, val=val: eng.memset(t[:], val),
             inc="v")
    t_conj = p.tok("v")
    t_ident = p.dma("sync", "c0", ident[:], ident_in)
    t_mask = p.dma("sync", "c0", mask[:], mask_in)
    t_lnw = p.dma("sync", "c0", lnw_sb[:], lnw_in.rearrange("a p -> p a"),
                  slow=True)
    t_consts = ("c0", p.counts["c0"])
    t_ident = t_mask = t_lnw = t_consts

    a_era = ExitStack()
    xn2_fm = a_era.enter_context(
        nc.sbuf_tensor("xn2_fm", [128, NIC, S], BF16, side="left"))
    a1 = ExitStack()
    a1sb = lambda name, shape, dtype=F32: a1.enter_context(
        nc.sbuf_tensor(name, shape, dtype, side="right"))
    xbuf = [a1sb(f"xbuf{i}", [128, D]) for i in range(2)]
    xcb = a1sb("xcb", [128, D])
    xnb = a1sb("xnb", [128, D])
    rrb = a1sb("rrb", [128, D])
    xn2c = [a1sb(f"xn2c{i}", [128, D], BF16) for i in range(2)]
    xqc = [a1sb(f"xqc{i}", [128, D], BF16) for i in range(2)]
    stat = a1sb("stat", [128, 8])
    xq_col = [a1sb(f"xqcol{i}", [128, NIC, 128], BF16) for i in range(2)]

    xbuf_tok = [None, None]
    xn2c_tok = [None, None]
    xqc_tok = [None, None]
    xqcol_tok = [None, None]
    t_xn2fm = []
    t_xqsp = []
    for tc in range(NTC):
        bi = tc % 2
        xb = xbuf[bi]
        tl = p.dma("sync", "xa" if bi == 0 else "xb", xb[:],
                   x_in[tc * 128:(tc + 1) * 128, :],
                   waits=[xbuf_tok[bi]])
        p.op("vector",
             lambda eng, xb=xb: eng.tensor_scalar(
                 dump[:, 0:D], xb[:], 1.0, 0.0, ALU.mult, ALU.add,
                 accum_out=stat[:, 0:1]),
             waits=[tl], inc="v")
        vpad()
        p.op("vector",
             lambda eng: eng.tensor_scalar(
                 stat[:, 1:2], stat[:, 0:1], 1.0 / D, None, ALU.mult),
             inc="v")
        vpad()
        t_xc = p.op("vector",
                    lambda eng, xb=xb: eng.tensor_scalar(
                        xcb[:], xb[:], stat[:, 1:2], None, ALU.subtract),
                    inc="v")
        xbuf_tok[bi] = t_xc
        t_vs = p.op("scalar",
                    lambda eng: eng.activation(
                        dump_sc[:, 0:D], xcb[:], AF.Square,
                        accum_out=stat[:, 2:3]),
                    waits=[t_xc], inc="sc")
        spad()
        t_sd = p.op("scalar",
                    lambda eng: eng.activation(
                        stat[:, 3:4], stat[:, 2:3], AF.Sqrt,
                        bias=c_eps5[:, 0:1], scale=1.0 / D),
                    waits=[t_conj], inc="sc")
        t_rstd = p.op("vector",
                      lambda eng: eng.reciprocal(stat[:, 4:5], stat[:, 3:4]),
                      waits=[t_sd], inc="v")
        vpad()
        t_xn = p.op("vector",
                    lambda eng: eng.tensor_scalar(
                        xnb[:], xcb[:], stat[:, 4:5], None, ALU.mult),
                    inc="v")
        t_r2s = p.op("scalar",
                     lambda eng: eng.activation(
                         dump_sc[:, 0:D], xnb[:], AF.Square,
                         accum_out=stat[:, 2:3]),
                     waits=[t_xn], inc="sc")
        spad()
        t_sd2 = p.op("scalar",
                     lambda eng: eng.activation(
                         stat[:, 3:4], stat[:, 2:3], AF.Sqrt,
                         bias=c_eps6[:, 0:1], scale=1.0 / D),
                     inc="sc")
        t_r2 = p.op("vector",
                    lambda eng: eng.reciprocal(stat[:, 4:5], stat[:, 3:4]),
                    waits=[t_sd2], inc="v")
        vpad()
        t_xn2 = p.op("vector",
                     lambda eng, o=xn2c[bi]: eng.tensor_scalar(
                         o[:], xnb[:], stat[:, 4:5], None, ALU.mult),
                     waits=[xn2c_tok[bi]], inc="v")
        p.op("vector",
             lambda eng: eng.reduce_max(
                 stat[:, 5:6], xnb[:], axis=mybir.AxisListType.X,
                 apply_absolute_value=True),
             inc="v")
        vpad()
        p.op("vector",
             lambda eng, tc=tc: eng.tensor_scalar(
                 s_all[:, tc:tc + 1], stat[:, 5:6], 1.0 / 127.0, 1e-5,
                 ALU.mult, ALU.add),
             inc="v")
        vpad()
        t_rs = p.op("vector",
                    lambda eng, tc=tc: eng.reciprocal(
                        stat[:, 7:8], s_all[:, tc:tc + 1]),
                    inc="v")
        t_rr = p.op("scalar",
                    lambda eng: eng.activation(
                        rrb[:], xnb[:], AF.Copy, bias=C2P23,
                        scale=stat[:, 7:8]),
                    waits=[t_rs, t_conj], inc="sc")
        t_xq = p.op("vector",
                    lambda eng, o=xqc[bi]: eng.tensor_scalar(
                        o[:], rrb[:], C2P23, None, ALU.subtract),
                    waits=[t_rr, xqc_tok[bi]], inc="v")
        if tc == 0 and os.environ.get("KDBG"):
            dts = [p.dma("sync", "dbq", dbg_sp[:, 0:D], xbuf[bi][:], waits=[t_xq]),
                   p.dma("sync", "dbq", dbg_sp[:, D:2 * D], xcb[:], waits=[t_xq]),
                   p.dma("sync", "dbq", dbg_sp[:, 2 * D:3 * D], xnb[:],
                         waits=[t_xq]),
                   p.dma("sync", "dbq", dbg_sp[:, 3 * D:4 * D], rrb[:],
                         waits=[t_xq]),
                   p.dma("sync", "dbq", dbg_sp[:, 4 * D:4 * D + 8], stat[:, 0:8],
                         waits=[t_xq])]
            p.op("vector", None, waits=dts)
            p.op("scalar", None, waits=dts)
        t_x1 = p.dma("sync", "na" if bi == 0 else "nb",
                     xn2_fm[:, :, tc * 128:(tc + 1) * 128],
                     xn2c[bi].ap().rearrange("p (a b) -> p a b", b=128),
                     waits=[t_xn2], transpose=True)
        xn2c_tok[bi] = t_x1
        t_x2 = p.dma("sync", "qca" if bi == 0 else "qcb", xq_col[bi][:],
                     xqc[bi].ap().rearrange("p (a b) -> p a b", b=128),
                     waits=[t_xq, xqcol_tok[bi]], transpose=True)
        xqc_tok[bi] = t_x2
        t_sp = p.dma("gpsimd", "xsa" if bi == 0 else "xsb",
                     xq_sp.rearrange("(a p) t -> p a t", p=128)[
                         :, :, tc * 128:(tc + 1) * 128],
                     xq_col[bi][:], waits=[t_x2])
        xqcol_tok[bi] = t_sp
        t_xn2fm.append(t_x1)
        t_xqsp.append(t_sp)
    t_xqsp_tot = [("xsa", p.counts["xsa"]), ("xsb", p.counts["xsb"])]
    a1.close()
    p.phase_barrier()

    f0 = ExitStack()
    wbuf = [f0.enter_context(nc.sbuf_tensor(f"f0w{i}", [128, I], F32,
                                                side="right"))
            for i in range(2)]
    f0dump = f0.enter_context(nc.sbuf_tensor("f0dump", [128, I], BF16,
                                             side="right"))
    ps_ws = f0.enter_context(nc.psum_tensor("ps_ws", [128, 4], F32))
    ws_row = f0.enter_context(nc.sbuf_tensor("ws_row", [1, 4], F32,
                                                 side="right"))
    rws_row = f0.enter_context(nc.sbuf_tensor("rws_row", [1, 4], F32,
                                              side="right"))
    srcs = [
        (wgT_in.rearrange("(a p) c -> p a c", p=128), NIC, 0),
        (wuT_in.rearrange("(a p) c -> p a c", p=128), NIC, 16),
        (wdT_in.rearrange("(c p) d -> p c d", p=128), 16, 32),
    ]
    buf_tok = [None, None]
    t_sum_toks = []
    idx = 0
    for mi, (src, nch, base) in enumerate(srcs):
        for c in range(nch):
            bi = idx % 2
            if mi < 2:
                chunk = src[:, c, :]
                dst = wbuf[bi][:]
            else:
                chunk = src[:, c * 4:(c + 1) * 4, :]
                dst = wbuf[bi].ap().rearrange("p (a b) -> p a b", b=2048)
            tl = p.dma("gpsimd", "f0a" if bi == 0 else "f0b", dst, chunk,
                       waits=[buf_tok[bi]])
            buf_tok[bi] = p.op(
                "scalar",
                lambda eng, b=wbuf[bi], k=base + c: eng.activation(
                    f0dump[:], b[:], AF.Abs,
                    accum_out=wsacc[:, k:k + 1]),
                waits=[tl], inc="sc")
            idx += 1
    t_red = []
    for mi, (lo, hi) in enumerate(((0, 16), (16, 32), (32, 48))):
        t_red.append(p.op(
            "vector",
            lambda eng, mi=mi, lo=lo, hi=hi: eng.reduce_sum(
                wsacc[:, 48 + mi:49 + mi], wsacc[:, lo:hi],
                axis=mybir.AxisListType.X),
            waits=[buf_tok[0], buf_tok[1]], inc="v"))
    t_mm_ws = []
    for mi in range(3):
        t_mm_ws.append(p.op(
            "tensor",
            lambda eng, mi=mi: eng.matmul(
                ps_ws[0:1, mi:mi + 1], ones_col[:, 0:1],
                wsacc[:, 48 + mi:49 + mi], start=True, stop=True),
            waits=[t_red[mi], t_conj], inc="te"))
    t_wsr = p.op("vector",
                 lambda eng: eng.tensor_scalar(
                     ws_row[0:1, 0:3], ps_ws[0:1, 0:3], 1.0 / 16777216.0,
                     1e-5, ALU.mult, ALU.add),
                 waits=t_mm_ws, inc="v")
    vpad()
    t_rwsr = p.op("vector",
                  lambda eng: eng.reciprocal(rws_row[0:1, 0:3],
                                             ws_row[0:1, 0:3]),
                  waits=[t_wsr], inc="v")
    t_bc1 = p.op("tensor",
                 lambda eng: eng.matmul(ps_ws[:, 0:3], ones_row[0:1, :],
                                        ws_row[0:1, 0:3], start=True,
                                        stop=True),
                 waits=[t_rwsr, t_conj], inc="te")
    t_wsb = p.op("vector",
                 lambda eng: eng.tensor_copy(ws_vec[:, 0:3], ps_ws[:, 0:3]),
                 waits=[t_bc1], inc="v")
    t_bc2 = p.op("tensor",
                 lambda eng: eng.matmul(ps_ws[:, 0:3], ones_row[0:1, :],
                                        rws_row[0:1, 0:3], start=True,
                                        stop=True),
                 waits=[t_wsb], inc="te")
    t_rwsb = p.op("vector",
                  lambda eng: eng.tensor_copy(rws_vec[:, 0:3],
                                              ps_ws[:, 0:3]),
                  waits=[t_bc2], inc="v")

    t_nA = []
    t_nA.append(p.op("vector",
                     lambda eng: eng.tensor_scalar(
                         negA_g[:], s_all[:], ws_vec[:, 0:1], -1.0, ALU.mult,
                         ALU.mult),
                     waits=[t_wsb], inc="v"))
    t_nA.append(p.op("vector",
                     lambda eng: eng.tensor_scalar(
                         negA_u[:], s_all[:], ws_vec[:, 1:2], -1.0, ALU.mult,
                         ALU.mult),
                     inc="v"))
    f0.close()

    p.phase_barrier()

    a2 = ExitStack()
    a2sb = lambda name, shape, dtype=F32: a2.enter_context(
        nc.sbuf_tensor(name, shape, dtype, side="right"))
    wqf = [a2sb(f"wqf{i}", [128, NIC, 512]) for i in range(2)]
    wqb = a2sb("wqb", [128, NIC, 512], BF16)
    ropeC = [a2sb(f"ropeC{i}", [128, 64]) for i in range(2)]
    ropeS = [a2sb(f"ropeS{i}", [128, 64]) for i in range(2)]
    rope_sc = [a2sb(f"ropesc{i}", [128, 256]) for i in range(4)]
    qrot = [a2sb(f"qrot{i}", [128, 512], BF16) for i in range(2)]
    ps_q = [a2.enter_context(nc.psum_tensor(f"ps_q{i}", [128, 512], F32))
            for i in range(2)]

    def emit_rope(ps, nheads, Cb, Sb, scr, out_bf, waits):
        ve = ps.ap().rearrange("p (h c) -> p h c", c=128)[:, :, 0:64]
        vo = ps.ap().rearrange("p (h c) -> p h c", c=128)[:, :, 64:128]
        Cb3 = bcast_mid(Cb, nheads)
        Sb3 = bcast_mid(Sb, nheads)
        vws = [scr[i].ap().rearrange("p (h c) -> p h c", c=64)[:, 0:nheads]
               for i in range(4)]
        oe = out_bf.ap().rearrange("p (h c) -> p h c", c=128)[:, :, 0:64]
        oo = out_bf.ap().rearrange("p (h c) -> p h c", c=128)[:, :, 64:128]
        p.op("vector", lambda eng: eng.tensor_tensor(vws[0], ve, Cb3,
                                                     ALU.mult),
             waits=waits, inc="v")
        p.op("vector", lambda eng: eng.tensor_tensor(vws[1], vo, Sb3,
                                                     ALU.mult), inc="v")
        p.op("vector", lambda eng: eng.tensor_tensor(vws[2], ve, Sb3,
                                                     ALU.mult), inc="v")
        p.op("vector", lambda eng: eng.tensor_tensor(vws[3], vo, Cb3,
                                                     ALU.mult), inc="v")
        p.op("vector", lambda eng: eng.tensor_tensor(oe, vws[0], vws[1],
                                                     ALU.subtract), inc="v")
        return p.op("vector", lambda eng: eng.tensor_tensor(oo, vws[2],
                                                            vws[3], ALU.add),
                    inc="v")

    wq_src = wq_in.rearrange("(a p) o -> p a o", p=128)
    ropeb_tok = [None, None]
    qrot_tok = [None, None]
    psq_tok = [None, None]
    wqf_tok = [None, None]
    t_last_qmm = None
    t_qsp = [None] * NTC
    for qo in range(4):
        wi = qo % 2
        twl = p.dma("gpsimd", "wqa" if wi == 0 else "wqb", wqf[wi][:],
                    wq_src[:, :, qo * 512:(qo + 1) * 512],
                    waits=[wqf_tok[wi]])
        casts = []
        for ic in range(NIC):
            casts.append(p.op(
                "vector",
                lambda eng, ic=ic, wi=wi: eng.tensor_scalar(
                    wqb[:, ic, :], wqf[wi][:, ic, :], lnw_sb[:, ic:ic + 1],
                    None, ALU.mult),
                waits=[twl, t_lnw, t_last_qmm, t_nA[-1]], inc="v"))
        wqf_tok[wi] = casts[-1]
        t_cast = casts[-1]
        for tc in range(NTC):
            ri = tc % 2
            pi = tc % 2
            rsem = "rpa" if ri == 0 else "rpb"
            tC = p.dma("sync", rsem, ropeC[ri][:],
                       ropeC_in[tc * 128:(tc + 1) * 128, :],
                       waits=[ropeb_tok[ri]])
            tS = p.dma("sync", rsem, ropeS[ri][:],
                       ropeS_in[tc * 128:(tc + 1) * 128, :])
            mm_last = None
            for ic in range(NIC):
                mm_last = p.op(
                    "tensor",
                    lambda eng, ic=ic, tc=tc, pi=pi: eng.matmul(
                        ps_q[pi][:], xn2_fm[:, ic, tc * 128:(tc + 1) * 128],
                        wqb[:, ic, :], start=(ic == 0), stop=(ic == NIC - 1)),
                    waits=([t_cast, psq_tok[pi], t_xn2fm[tc]]
                           if ic == 0 else ()),
                    inc="te")
            t_last_qmm = mm_last
            t_rope = emit_rope(ps_q[pi], 4, ropeC[ri][:], ropeS[ri][:],
                               rope_sc, qrot[pi],
                               [mm_last, tC, tS, qrot_tok[pi]])
            ropeb_tok[ri] = t_rope
            psq_tok[pi] = t_rope
            tqs = p.dma("gpsimd", "qspa" if pi == 0 else "qspb",
                        q_sp[tc * 128:(tc + 1) * 128,
                             qo * 512:(qo + 1) * 512],
                        qrot[pi][:], waits=[t_rope])
            qrot_tok[pi] = tqs
            t_qsp[tc] = tqs
    a2.close()

    p.phase_barrier()

    a3 = ExitStack()
    a3sb = lambda name, shape, dtype=F32: a3.enter_context(
        nc.sbuf_tensor(name, shape, dtype, side="right"))
    kv_era = ExitStack()
    k_fm = kv_era.enter_context(
        nc.sbuf_tensor("k_fm", [128, KH, S], BF16, side="right"))
    v_tm = kv_era.enter_context(
        nc.sbuf_tensor("v_tm", [128, NTC, KH * HD], BF16, side="right"))
    wkf = a3sb("wkf", [128, NIC, 512])
    wkb = a3sb("wkb", [128, NIC, 512], BF16)
    wvb = a3sb("wvb", [128, NIC, 512], BF16)
    ropeCk = [a3sb(f"ropeCk{i}", [128, 64]) for i in range(2)]
    ropeSk = [a3sb(f"ropeSk{i}", [128, 64]) for i in range(2)]
    rope_sck = [a3sb(f"ropesck{i}", [128, 256]) for i in range(4)]
    krot = [a3sb(f"krot{i}", [128, 512], BF16) for i in range(2)]
    ps_k = [a3.enter_context(nc.psum_tensor(f"ps_k{i}", [128, 512], F32))
            for i in range(2)]
    ps_v = [a3.enter_context(nc.psum_tensor(f"ps_v{i}", [128, 512], F32))
            for i in range(2)]

    twk = p.dma("gpsimd", "wkv", wkf[:], wk_in.rearrange("(a p) o -> p a o", p=128))
    ckl = None
    for ic in range(NIC):
        ckl = p.op(
            "vector",
            lambda eng, ic=ic: eng.tensor_scalar(
                wkb[:, ic, :], wkf[:, ic, :], lnw_sb[:, ic:ic + 1], None,
                ALU.mult),
            waits=[twk], inc="v")
    twv = p.dma("gpsimd", "wkv", wkf[:], wv_in.rearrange("(a p) o -> p a o", p=128),
                waits=[ckl])
    for ic in range(NIC):
        ckl = p.op(
            "vector",
            lambda eng, ic=ic: eng.tensor_scalar(
                wvb[:, ic, :], wkf[:, ic, :], lnw_sb[:, ic:ic + 1], None,
                ALU.mult),
            waits=[twv], inc="v")
    t_wkvb = ckl

    ropek_tok = [None, None]
    krot_tok = [None, None]
    psk_tok = [None, None]
    psv_tok = [None, None]
    kfm_tok = [None, None]
    t_kx_l = [None] * NTC
    t_vev_l = [None] * NTC
    for tc in range(NTC):
        ri = tc % 2
        pi = tc % 2
        rsem = "rpa" if ri == 0 else "rpb"
        tC = p.dma("sync", rsem, ropeCk[ri][:],
                   ropeC_in[tc * 128:(tc + 1) * 128, :],
                   waits=[ropek_tok[ri]])
        tS = p.dma("sync", rsem, ropeSk[ri][:],
                   ropeS_in[tc * 128:(tc + 1) * 128, :])
        mmk_last = None
        mmv_last = None
        for ic in range(NIC):
            mmk_last = p.op(
                "tensor",
                lambda eng, ic=ic, tc=tc, pi=pi: eng.matmul(
                    ps_k[pi][:], xn2_fm[:, ic, tc * 128:(tc + 1) * 128],
                    wkb[:, ic, :], start=(ic == 0), stop=(ic == NIC - 1)),
                waits=([t_wkvb, psk_tok[pi]] if ic == 0 else ()),
                inc="te")
            mmv_last = p.op(
                "tensor",
                lambda eng, ic=ic, tc=tc, pi=pi: eng.matmul(
                    ps_v[pi][:], xn2_fm[:, ic, tc * 128:(tc + 1) * 128],
                    wvb[:, ic, :], start=(ic == 0), stop=(ic == NIC - 1)),
                waits=([psv_tok[pi]] if ic == 0 else ()),
                inc="te")
        t_ropek = emit_rope(ps_k[pi], KH, ropeCk[ri][:], ropeSk[ri][:],
                            rope_sck, krot[pi],
                            [mmk_last, tC, tS, krot_tok[pi]])
        ropek_tok[ri] = t_ropek
        psk_tok[pi] = t_ropek
        t_vev = p.op("vector",
                     lambda eng, tc=tc, pi=pi: eng.tensor_copy(
                         v_tm[:, tc, :], ps_v[pi][:]),
                     waits=[mmv_last], inc="v")
        psv_tok[pi] = t_vev
        t_kx = p.dma("sync", "kfa" if pi == 0 else "kfb",
                     k_fm[:, :, tc * 128:(tc + 1) * 128],
                     krot[pi].ap().rearrange("p (a b) -> p a b", b=128),
                     waits=[t_ropek, kfm_tok[pi]], transpose=True)
        kfm_tok[pi] = t_kx
        krot_tok[pi] = t_kx
        t_kx_l[tc] = t_kx
        t_vev_l[tc] = t_vev
    t_a3_vfence = p.tok("v")
    a3.close()
    a_era.close()

    p.phase_barrier()

    b_era = ExitStack()
    attn_out = b_era.enter_context(
        nc.sbuf_tensor("attn_out", [128, H, S], BF16, side="left"))
    bph = ExitStack()
    bsb = lambda name, shape, dtype=F32: bph.enter_context(
        nc.sbuf_tensor(name, shape, dtype, side="right"))
    qc_tm = [bsb(f"qc_tm{i}", [128, D], BF16) for i in range(2)]
    qc_fm = [bsb(f"qc_fm{i}", [128, H, 128], BF16) for i in range(2)]
    sc_f32 = [bsb(f"sc_f32{i}", [128, 640]) for i in range(2)]
    neg_sc = [bsb(f"neg_sc{i}", [128, 640]) for i in range(2)]
    expb = [bsb(f"expb{i}", [128, 640]) for i in range(2)]
    smstat = bsb("smstat", [128, 4])
    probs = [bsb("probs0", [128, H, 5, 128], BF16)] * 2
    probsT = [bsb("probsT0", [128, H, 5, 128], BF16)] * 2
    ps_s = [bph.enter_context(nc.psum_tensor(f"ps_s{i}", [128, 1024], F32))
            for i in range(2)]
    ps_av = bph.enter_context(nc.psum_tensor("ps_av", [128, 2048], F32))

    pm = p.op("vector", lambda eng: eng.memset(
        probs[0].ap().rearrange("p a b c -> p (a b c)"), 0.0), inc="v")
    qctm_tok = [None, None]
    qcfm_tok = [None, None]
    probs_tok = [pm, pm]
    probsT_tok = [None, None]
    ps_s_tok = [None, None]
    ps_av_tok = None
    t_b_vlast = None
    for tc in range(NTC):
        bi = tc % 2
        nkc = min(tc + 1, 5)
        kc0 = tc - nkc + 1
        wcols = nkc * 128
        c0 = kc0 * 128
        moff = 640 - wcols
        tq = p.dma("sync", "qta" if bi == 0 else "qtb", qc_tm[bi][:],
                   q_sp[tc * 128:(tc + 1) * 128, :],
                   waits=[qctm_tok[bi], t_qsp[tc]])
        tqx = p.dma("sync", "qfa" if bi == 0 else "qfb", qc_fm[bi][:],
                    qc_tm[bi].ap().rearrange("p (a b) -> p a b", b=128),
                    waits=[tq, qcfm_tok[bi]], transpose=True)
        qctm_tok[bi] = tqx
        ev_toks = []

        def finalize(h, t_ex):
            hp = h % 2
            t_rd = p.op("vector",
                        lambda eng, hp=hp: eng.reciprocal(
                            smstat[:, 2 + hp:3 + hp],
                            smstat[:, 2 + hp:3 + hp]),
                        waits=[t_ex], inc="v")
            vpad()
            t_pr = p.op(
                "vector",
                lambda eng, h=h, hp=hp, bi=bi, nkc=nkc, wcols=wcols:
                eng.tensor_scalar(
                    probs[bi][:, h, 5 - nkc:5, :].rearrange(
                        "p a b -> p (a b)"),
                    expb[hp][:, 0:wcols], smstat[:, 2 + hp:3 + hp], None,
                    ALU.mult),
                waits=[probs_tok[bi]], inc="v")
            ev_toks.append(t_pr)

        prev = None
        for h in range(H):
            si = h % 2
            hp = h % 2
            kh = h // 4
            t_mm = []
            n_done = 0
            while n_done < wcols:
                n = min(512, wcols - n_done)
                first = (n_done == 0)
                t_mm.append(p.op(
                    "tensor",
                    lambda eng, h=h, kh=kh, si=si, n=n, nd=n_done, c0=c0,
                    bi=bi: eng.matmul(
                        ps_s[si][:, nd:nd + n], qc_fm[bi][:, h, :],
                        k_fm[:, kh, c0 + nd:c0 + nd + n],
                        start=True, stop=True),
                    waits=([tqx, ps_s_tok[si], t_kx_l[tc], t_vev_l[tc],
                            t_a3_vfence] if first else ()),
                    inc="te"))
                n_done += n
            t_ev = p.op(
                "vector",
                lambda eng, wcols=wcols, moff=moff, si=si, hp=hp:
                eng.tensor_tensor(
                    sc_f32[hp][:, 0:wcols], ps_s[si][:, 0:wcols],
                    mask[:, moff:moff + wcols], ALU.add),
                waits=t_mm + [t_mask], inc="v")
            ps_s_tok[si] = t_ev
            t_ng = p.op(
                "vector",
                lambda eng, wcols=wcols, hp=hp: eng.tensor_scalar(
                    neg_sc[hp][:, 0:wcols], sc_f32[hp][:, 0:wcols], -SCL,
                    3.0e38, ALU.mult, ALU.min,
                    accum_out=smstat[:, hp:hp + 1]),
                inc="v")
            t_ex = p.op(
                "scalar",
                lambda eng, wcols=wcols, hp=hp: eng.activation(
                    expb[hp][:, 0:wcols], neg_sc[hp][:, 0:wcols], AF.Exp,
                    bias=smstat[:, hp:hp + 1], scale=-1.0,
                    accum_out=smstat[:, 2 + hp:3 + hp]),
                waits=[t_ng], inc="sc")
            if prev is not None:
                finalize(*prev)
            prev = (h, t_ex)
        finalize(*prev)
        probs_tok[0] = probs_tok[1] = ev_toks[-1]
        qcfm_tok[bi] = t_mm[-1]
        t_px = p.dma("sync", "ptx", probsT[bi][:], probs[bi][:],
                     waits=ev_toks + [probsT_tok[bi]], transpose=True)
        probs_tok[0] = probs_tok[1] = t_px
        if os.environ.get("KDBG") and tc == int(os.environ.get("DBGTC", 15)):
            d1 = p.dma("sync", "dbq", dbg_ps,
                       probs[bi].ap().rearrange("p a b c -> p (a b c)"),
                       waits=[t_px])
            d2 = p.dma("sync", "dbq", dbg_pt,
                       probsT[bi].ap().rearrange("p a b c -> p (a b c)"),
                       waits=[t_px])
            p.op("vector", None, waits=[d1, d2])
            p.op("tensor", None, waits=[d1, d2])
        t_avmm = None
        for h in range(H):
            kh = h // 4
            for i_kc in range(nkc):
                kci = 5 - nkc + i_kc
                kc_abs = kc0 + i_kc
                first = (h == 0 and i_kc == 0)
                t_avmm = p.op(
                    "tensor",
                    lambda eng, h=h, kh=kh, kci=kci, kc_abs=kc_abs, bi=bi,
                    i_kc=i_kc, nkc=nkc: eng.matmul(
                        ps_av[:, h * 128:(h + 1) * 128],
                        v_tm[:, kc_abs, kh * 128:(kh + 1) * 128],
                        probsT[bi][:, h, kci, :],
                        start=(i_kc == 0), stop=(i_kc == nkc - 1)),
                    waits=([t_px, ps_av_tok] if first else ()),
                    inc="te")
        probsT_tok[0] = probsT_tok[1] = t_avmm
        evs = []
        for g in range(4):
            evs.append(p.op(
                "vector",
                lambda eng, g=g, tc=tc: eng.tensor_copy(
                    attn_out[:, 4 * g:4 * (g + 1), tc * 128:(tc + 1) * 128],
                    ps_av[:, g * 512:(g + 1) * 512].rearrange(
                        "p (a b) -> p a b", b=128)),
                waits=[t_avmm], inc="v"))
        ps_av_tok = evs[-1]
        t_b_vlast = evs[-1]
    if os.environ.get("KDBG"):
        p.dma("sync", "dbq", dbg_ao, attn_out.ap().rearrange("p a b -> p (a b)"),
              waits=[t_b_vlast])
        p.dma("sync", "dbq", dbg_kf, k_fm.ap().rearrange("p a b -> p (a b)"),
              waits=[t_b_vlast])
        p.dma("sync", "dbq", dbg_vt, v_tm.ap().rearrange("p a b -> p (a b)"),
              waits=[t_b_vlast])
    bph.close()
    kv_era.close()

    p.phase_barrier()

    wop = ExitStack()
    wsb_ = lambda name, shape, dtype=F32: wop.enter_context(
        nc.sbuf_tensor(name, shape, dtype, side="right"))
    wof = [wsb_(f"wof{i}", [128, D]) for i in range(2)]
    wob = wsb_("wob", [128, NDC, D], BF16)
    xfb = [wsb_(f"xfb{i}", [128, 512]) for i in range(2)]
    pblk = [wsb_(f"pblk{i}", [128, 512]) for i in range(2)]
    ps_wo = [wop.enter_context(nc.psum_tensor(f"ps_wo{i}", [128, 512], F32))
             for i in range(2)]
    wof_tok = [None, None]
    t_wob = None
    for ic in range(NIC):
        wi = ic % 2
        twl = p.dma("gpsimd", "woa" if wi == 0 else "wob_", wof[wi][:],
                    wo_in[ic * 128:(ic + 1) * 128, :], waits=[wof_tok[wi]])
        t_wob = p.op("vector",
                     lambda eng, ic=ic, wi=wi: eng.tensor_copy(
                         wob[:, ic, :], wof[wi][:]),
                     waits=[twl], inc="v")
        wof_tok[wi] = t_wob
    xfb_tok = [None, None]
    pblk_tok = [None, None]
    ps_wo_tok = [None, None]
    t_part = [[None] * 4 for _ in range(NDC)]
    for dc in range(NDC):
        for tt in range(4):
            pi = (dc * 4 + tt) % 2
            mm_last = None
            for h in range(H):
                mm_last = p.op(
                    "tensor",
                    lambda eng, h=h, dc=dc, tt=tt, pi=pi: eng.matmul(
                        ps_wo[pi][:], wob[:, h, dc * 128:(dc + 1) * 128],
                        attn_out[:, h, tt * 512:(tt + 1) * 512],
                        start=(h == 0), stop=(h == H - 1)),
                    waits=([t_wob, ps_wo_tok[pi], t_b_vlast]
                           if h == 0 else ()),
                    inc="te")
            if os.environ.get("KDBG2") and dc == 0 and tt == 0:
                dw = p.dma("sync", "dbq", dbg_wob,
                           wob.ap().rearrange("p a b -> p (a b)"),
                           waits=[t_wob])
                for e in ("tensor", "vector", "scalar", "gpsimd"):
                    p.op(e, None, waits=[dw])
            tx = p.dma("sync", "xfa" if pi == 0 else "xfb_", xfb[pi][:],
                       xT_in[dc * 128:(dc + 1) * 128,
                             tt * 512:(tt + 1) * 512],
                       waits=[xfb_tok[pi]])
            t_add = p.op(
                "vector",
                lambda eng, pi=pi: eng.tensor_tensor(
                    pblk[pi][:], ps_wo[pi][:], xfb[pi][:], ALU.add),
                waits=[mm_last, tx, pblk_tok[pi]], inc="v")
            ps_wo_tok[pi] = t_add
            xfb_tok[pi] = t_add
            t_st = p.dma("sync", "pta" if pi == 0 else "ptb",
                         part_sp[dc * 128:(dc + 1) * 128,
                                 tt * 512:(tt + 1) * 512],
                         pblk[pi][:], waits=[t_add])
            pblk_tok[pi] = t_st
            t_part[dc][tt] = t_st
    t_wo_vfence = p.tok("v")
    wop.close()
    b_era.close()

    p.phase_barrier()

    f1 = ExitStack()
    fsb = lambda name, shape, dtype=F32: f1.enter_context(
        nc.sbuf_tensor(name, shape, dtype, side="right"))
    xq_rd = fsb("xq_rd", [128, NIC, S], BF16)
    wgf = fsb("wgf", [128, NIC, 512])
    wgq = [fsb(f"wgq{i}", [128, NIC, 512], BF16) for i in range(2)]
    wuq = [fsb(f"wuq{i}", [128, NIC, 512], BF16) for i in range(2)]
    wdq = fsb("wdq", [128, 1, D], BF16)
    tq1 = fsb("tq1", [128, NIC * 256])
    sgb = [fsb(f"sgb{i}", [128, 512]) for i in range(2)]
    hbl = [fsb(f"hbl{i}", [128, 512]) for i in range(2)]
    ps_g = [f1.enter_context(nc.psum_tensor(f"ps_g{i}", [128, 512], F32))
            for i in range(2)]
    ps_u = [f1.enter_context(nc.psum_tensor(f"ps_u{i}", [128, 512], F32))
            for i in range(2)]

    t_xqrd = p.dma("gpsimd", "xqr", xq_rd[:],
                   xq_sp.rearrange("(a p) t -> p a t", p=128),
                   waits=t_xqsp_tot)

    tq1_tok = [None]

    def emit_quant(src_f32, dst_bf, rws_slice, waits, nfree):
        t1 = p.op("scalar",
                  lambda eng, src=src_f32, nf=nfree, r=rws_slice:
                  eng.activation(tq1[:, 0:nf], src, AF.Copy, bias=C2P23,
                                 scale=r),
                  waits=list(waits) + [t_conj, t_rwsb, tq1_tok[0]],
                  inc="sc")
        t2 = p.op("scalar",
                  lambda eng, nf=nfree: eng.activation(
                      tq1[:, 0:nf], tq1[:, 0:nf], AF.Relu,
                      bias=c_2p23p1[:, 0:1], scale=-1.0),
                  inc="sc")
        t3 = p.op("vector",
                  lambda eng, dst=dst_bf, nf=nfree: eng.tensor_scalar(
                      dst, tq1[:, 0:nf], 2.0, 1.0, ALU.min, ALU.subtract),
                  waits=[t2], inc="v")
        tq1_tok[0] = t3
        return t3

    h_dst = h_sp.rearrange("(a p) o -> p a o", p=128)
    wgT_src = wgT_in.rearrange("(a p) c -> p a c", p=128)
    wuT_src = wuT_in.rearrange("(a p) c -> p a c", p=128)
    wdT_src = wdT_in.rearrange("(a p) c -> p a c", p=128)
    wdq_dst = wdq_sp.rearrange("(a p) c -> p a c", p=128)
    wgq_tok = [None, None]
    wuq_tok = [None, None]
    wdq_tok = [None]
    sgb_tok = [None, None]
    hbl_tok = [None, None]
    ps_g_tok = [None, None]
    ps_u_tok = [None, None]
    t_h_last = None
    t_wdq_last = None
    for ot in range(NOC):
        oi = ot % 2
        tg = p.dma("gpsimd", "wgl", wgf[:],
                   wgT_src[:, :, ot * 512:(ot + 1) * 512],
                   waits=[tq1_tok[0]])
        for hv2 in range(2):
            tgq = emit_quant(
                wgf.ap()[:, hv2 * 8:(hv2 + 1) * 8, :].rearrange(
                    "p a c -> p (a c)"),
                wgq[oi].ap()[:, hv2 * 8:(hv2 + 1) * 8, :].rearrange(
                    "p a c -> p (a c)"),
                rws_vec[:, 0:1], [tg, wgq_tok[oi]], NIC * 256)
        tu = p.dma("gpsimd", "wgl", wgf[:],
                   wuT_src[:, :, ot * 512:(ot + 1) * 512],
                   waits=[tgq])
        for hv2 in range(2):
            tuq = emit_quant(
                wgf.ap()[:, hv2 * 8:(hv2 + 1) * 8, :].rearrange(
                    "p a c -> p (a c)"),
                wuq[oi].ap()[:, hv2 * 8:(hv2 + 1) * 8, :].rearrange(
                    "p a c -> p (a c)"),
                rws_vec[:, 1:2], [tu, wuq_tok[oi]], NIC * 256)
        wgf4 = wgf.ap().rearrange("p a c -> p (a c)").rearrange(
            "p (a c) -> p a c", c=D)
        td = p.dma("gpsimd", "wgl", wgf4,
                   wdT_src[:, ot * 4:(ot + 1) * 4, :],
                   waits=[tq1_tok[0], tuq])
        for hv in range(4):
            tdq = emit_quant(
                wgf4[:, hv:hv + 1, :].rearrange("p a c -> p (a c)"),
                wdq.ap().rearrange("p a c -> p (a c)"),
                rws_vec[:, 2:3], [td, wdq_tok[0]], D)
            tds = p.dma("sync", "wds",
                        wdq_dst[:, ot * 4 + hv:ot * 4 + hv + 1, :],
                        wdq[:], waits=[tdq])
            wdq_tok[0] = tds
            t_wdq_last = tds
        mg_last = None
        mu_last = None
        for tc in range(NTC):
            pi = tc % 2
            for ic in range(NIC):
                mg_last = p.op(
                    "tensor",
                    lambda eng, ic=ic, tc=tc, pi=pi, oi=oi: eng.matmul(
                        ps_g[pi][:], xq_rd[:, ic, tc * 128:(tc + 1) * 128],
                        wgq[oi][:, ic, :], start=(ic == 0),
                        stop=(ic == NIC - 1)),
                    waits=([tgq, t_xqrd, ps_g_tok[pi]] if ic == 0 else ()),
                    inc="te")
                mu_last = p.op(
                    "tensor",
                    lambda eng, ic=ic, tc=tc, pi=pi, oi=oi: eng.matmul(
                        ps_u[pi][:], xq_rd[:, ic, tc * 128:(tc + 1) * 128],
                        wuq[oi][:, ic, :], start=(ic == 0),
                        stop=(ic == NIC - 1)),
                    waits=([tuq, ps_u_tok[pi]] if ic == 0 else ()),
                    inc="te")
            t_sg = p.op(
                "scalar",
                lambda eng, tc=tc, pi=pi: eng.activation(
                    sgb[pi][:], ps_g[pi][:], AF.Silu,
                    scale=negA_g[:, tc:tc + 1]),
                waits=[mg_last, t_nA[0], sgb_tok[pi]], inc="sc")
            ps_g_tok[pi] = t_sg
            t_h = p.op(
                "vector",
                lambda eng, tc=tc, pi=pi: eng.scalar_tensor_tensor(
                    hbl[pi][:], ps_u[pi][:], negA_u[:, tc:tc + 1], sgb[pi][:],
                    ALU.mult, ALU.mult),
                waits=[mu_last, t_sg, t_nA[1], hbl_tok[pi]], inc="v")
            ps_u_tok[pi] = t_h
            sgb_tok[pi] = t_h
            t_hs = p.dma("sync", "hsa" if pi == 0 else "hsb",
                         h_dst[:, tc, ot * 512:(ot + 1) * 512],
                         hbl[pi][:], waits=[t_h])
            hbl_tok[pi] = t_hs
            t_h_last = t_hs
        wgq_tok[oi] = mg_last
        wuq_tok[oi] = mu_last
    t_f1_vfence = p.tok("v")
    t_hs_tot = [("hsa", p.counts["hsa"]), ("hsb", p.counts["hsb"])]
    f1.close()

    p.phase_barrier()

    t_f1_scfence = p.tok("sc")
    f3 = ExitStack()
    f3sb = lambda name, shape, dtype=F32: f3.enter_context(
        nc.sbuf_tensor(name, shape, dtype, side="right"))
    h_qT = f3sb("h_qT", [128, NJC, 512], BF16)
    hrow = [f3sb("hrow0", [128, I])]
    hrow.append(hrow[0])
    hq_tm = f3sb("hq_tm", [128, I], BF16)
    rr3 = f3sb("rr3", [128, I])
    hstat = f3sb("hstat", [128, 2])
    hs_row = f3sb("hs_row", [1, 512])
    hsb_sb = f3sb("hsb_sb", [128, 512])
    wdb = [f3sb(f"wdb{i}", [128, 8, 1024], BF16) for i in range(2)]
    yblk = [f3sb(f"yblk{i}", [128, 512]) for i in range(2)]
    pfb = [f3sb(f"pfb{i}", [128, 512]) for i in range(2)]
    ps_y = [f3.enter_context(nc.psum_tensor(f"ps_y{i}", [128, 512], F32))
            for i in range(8)]

    h_src = h_sp.rearrange("(a p) o -> p a o", p=128)
    hrow_tok = [None, None]
    hq_tok = None
    hqT_wr_gate = None
    t_hr_prev = []
    t_dn_vlast = t_f1_vfence
    yblk_tok = [None, None]
    pfb_tok = [None, None]
    wdb_tok = [None, None]
    for tt in range(4):
        hx_toks = []
        hs_toks = []
        for sub in range(4):
            tc = tt * 4 + sub
            bi = 0
            tl = p.dma("gpsimd", "hrl", hrow[bi][:], h_src[:, tc, :],
                       waits=[hrow_tok[bi]] +
                             (t_hs_tot if (tt == 0 and sub == 0) else []))
            t_mx = p.op("vector",
                        lambda eng, bi=bi: eng.reduce_max(
                            hstat[:, 0:1], hrow[bi][:],
                            axis=mybir.AxisListType.X,
                            apply_absolute_value=True),
                        waits=[tl], inc="v")
            vpad()
            t_hsv = p.op("vector",
                         lambda eng, sub=sub: eng.tensor_scalar(
                             hs_all[:, sub:sub + 1], hstat[:, 0:1],
                             1.0 / 127.0, 1e-5, ALU.mult, ALU.add),
                         waits=t_hr_prev, inc="v")
            hs_toks.append(t_hsv)
            vpad()
            t_rhs = p.op("vector",
                         lambda eng, sub=sub: eng.reciprocal(
                             hstat[:, 1:2], hs_all[:, sub:sub + 1]),
                         inc="v")
            t_rr = p.op("scalar",
                        lambda eng, bi=bi: eng.activation(
                            rr3[:], hrow[bi][:], AF.Copy, bias=C2P23,
                            scale=hstat[:, 1:2]),
                        waits=[t_rhs, t_conj], inc="sc")
            hrow_tok[bi] = t_rr
            t_hq = p.op("vector",
                        lambda eng: eng.tensor_scalar(
                            hq_tm[:], rr3[:], C2P23, None, ALU.subtract),
                        waits=[t_rr, hq_tok], inc="v")
            t_hx = p.dma("sync", "hqx",
                         h_qT[:, :, sub * 128:(sub + 1) * 128],
                         hq_tm.ap().rearrange("p (a b) -> p a b", b=128),
                         waits=[t_hq, hqT_wr_gate], transpose=True)
            hq_tok = t_hx
            hx_toks.append(t_hx)
        t_hr = []
        for sub in range(4):
            t_hr.append(p.op(
                "tensor",
                lambda eng, sub=sub: eng.matmul(
                    ps_y[0][0:1, sub * 128:(sub + 1) * 128],
                    hs_all[:, sub:sub + 1], ident[:], start=True, stop=True),
                waits=[hs_toks[sub], t_ident, t_dn_vlast, t_f1_scfence],
                inc="te"))
        t_hr_prev = t_hr
        t_hrow = p.op("vector",
                      lambda eng: eng.tensor_scalar(
                          hs_row[0:1, :], ps_y[0][0:1, :], ws_vec[0:1, 2:3],
                          None, ALU.mult),
                      waits=t_hr + [t_wsb], inc="v")
        t_hbc = p.op("tensor",
                     lambda eng: eng.matmul(ps_y[1][:], ones_row[0:1, :],
                                            hs_row[0:1, :], start=True,
                                            stop=True),
                     waits=[t_hrow, t_conj], inc="te")
        t_hsb = p.op("vector",
                     lambda eng: eng.tensor_copy(hsb_sb[:], ps_y[1][:]),
                     waits=[t_hbc], inc="v")
        wdq_src4 = wdq_sp.rearrange("(a p) c -> p a c", p=128)
        for dh in range(2):
            mm_last = [None] * 8
            for jg in range(NJC // 8):
                wi = jg % 2
                twd = p.dma("gpsimd", "wda" if wi == 0 else "wdb2",
                            wdb[wi][:],
                            wdq_src4[:, jg * 8:(jg + 1) * 8,
                                     dh * 1024:(dh + 1) * 1024],
                            waits=[wdb_tok[wi], t_wdq_last])
                for j8 in range(8):
                    jc = jg * 8 + j8
                    for i8 in range(8):
                        mm_last[i8] = p.op(
                            "tensor",
                            lambda eng, jc=jc, j8=j8, i8=i8, wi=wi: eng.matmul(
                                ps_y[i8][:],
                                wdb[wi][:, j8, i8 * 128:(i8 + 1) * 128],
                                h_qT[:, jc, :],
                                start=(jc == 0), stop=(jc == NJC - 1)),
                            waits=([twd, t_hsb, t_dn_vlast,
                                    hx_toks[-1] if jc == 0 else None]
                                   if (i8 == 0 and j8 == 0) else ()),
                            inc="te")
                wdb_tok[wi] = mm_last[7]
            for i8 in range(8):
                dc = dh * 8 + i8
                bi2 = i8 % 2
                tpf = p.dma("sync", "pfa" if bi2 == 0 else "pfb2",
                            pfb[bi2][:],
                            part_sp[dc * 128:(dc + 1) * 128,
                                    tt * 512:(tt + 1) * 512],
                            waits=[pfb_tok[bi2], t_part[dc][tt]])
                t_y1 = p.op(
                    "vector",
                    lambda eng, i8=i8, bi2=bi2: eng.tensor_tensor(
                        yblk[bi2][:], ps_y[i8][:], hsb_sb[:], ALU.mult),
                    waits=[mm_last[i8], t_hsb, yblk_tok[bi2]], inc="v")
                t_y2 = p.op(
                    "vector",
                    lambda eng, bi2=bi2: eng.tensor_tensor(
                        yblk[bi2][:], pfb[bi2][:], yblk[bi2][:],
                        ALU.subtract),
                    waits=[tpf], inc="v")
                pfb_tok[bi2] = t_y2
                t_o = p.dma("sync", "yoa" if bi2 == 0 else "yob",
                            y_out[dc * 128:(dc + 1) * 128,
                                  tt * 512:(tt + 1) * 512],
                            yblk[bi2][:], waits=[t_y2])
                yblk_tok[bi2] = t_o
            t_dn_vlast = p.tok("v")
            hqT_wr_gate = mm_last[7]
    f3.close()

    fin = [(s, p.counts[s]) for s in DMA_SEMS] + [
        ("v", p.counts["v"]), ("sc", p.counts["sc"]), ("te", p.counts["te"])]
    p.op("sync", None, waits=fin)
    p.op("gpsimd", None, waits=[(s, p.counts[s]) for s in DMA_SEMS])

    nums = sorted(h.num for h in p.sem.values())
    lo, hi = nums[0], nums[-1]
    nc.all_engine_barrier()
    nc.gpsimd.dma_reset(range(lo, hi + 1))
    nc.gpsimd.sem_clear(range(lo, hi + 1))
    nc.all_engine_barrier()

    with nc.Block() as block:
        p.emit(block)
    gstack.close()
    return nc


_ncache = {}


def _get_program():
    if "nc" not in _ncache:
        _ncache["nc"] = build_program()
    return _ncache["nc"]


def _rope_tables():
    e = (np.arange(0, HD, 2, dtype=np.float32) / np.float32(HD)).astype(
        np.float32)
    inv = (np.float32(1.0) / np.power(np.float32(10000.0), e)).astype(
        np.float32)
    ang = (np.arange(S, dtype=np.float32)[:, None] * inv[None, :]).astype(
        np.float32)
    return np.cos(ang).astype(np.float32), np.sin(ang).astype(np.float32)


def kernel(L0_attn_x, L0_attn_ln_w, L0_attn_wq, L0_attn_wk, L0_attn_wv,
           L0_attn_wo, L0_ffn_w_gate, L0_ffn_w_up, L0_ffn_w_down):
    x = np.ascontiguousarray(np.asarray(L0_attn_x, dtype=np.float32))
    lnw = np.ascontiguousarray(
        np.asarray(L0_attn_ln_w, dtype=np.float32).reshape(NIC, 128))
    permH = np.concatenate([np.arange(0, 128, 2), np.arange(1, 128, 2)])
    permq = np.concatenate([hh * 128 + permH for hh in range(H)])
    permk = np.concatenate([hh * 128 + permH for hh in range(KH)])
    wq = np.ascontiguousarray(np.asarray(L0_attn_wq, np.float32)[:, permq])
    wk = np.ascontiguousarray(np.asarray(L0_attn_wk, np.float32)[:, permk])
    wv = np.ascontiguousarray(np.asarray(L0_attn_wv, np.float32))
    wo = np.ascontiguousarray(np.asarray(L0_attn_wo, np.float32))
    wgT = np.ascontiguousarray(np.asarray(L0_ffn_w_gate, np.float32).T)
    wuT = np.ascontiguousarray(np.asarray(L0_ffn_w_up, np.float32).T)
    wdT = np.ascontiguousarray(np.asarray(L0_ffn_w_down, np.float32).T)
    pp = np.arange(128)[:, None]
    cc = np.arange(640)[None, :]
    mask = np.where((cc >= pp) & (cc <= pp + W), 0.0, NEG).astype(np.float32)
    ropeC, ropeS = _rope_tables()
    ident = np.eye(128, dtype=np.float32)

    nc = _get_program()
    common = dict(lnw=lnw, wq=wq, wk=wk, wv=wv, wo=wo, wgT=wgT, wuT=wuT,
                  wdT=wdT, mask=mask, ropeC=ropeC, ropeS=ropeS, ident=ident)
    in_maps = []
    for b in range(8):
        m = dict(common)
        m["x"] = np.ascontiguousarray(x[b])
        m["xT"] = np.ascontiguousarray(x[b].T)
        in_maps.append(m)
    res = run_bass_kernel_spmd(nc, in_maps, core_ids=list(range(8)))
    out = np.stack([np.asarray(r["y"], np.float32).T for r in res.results],
                   axis=0)
    return out


# version 07
# speedup vs baseline: 1.1340x; 1.1340x over previous
from contextlib import ExitStack

import numpy as np
import concourse.bass as bass
import concourse.mybir as mybir
from concourse.bass_utils import run_bass_kernel_spmd

F32 = mybir.dt.float32
BF16 = mybir.dt.bfloat16
AF = mybir.ActivationFunctionType
ALU = mybir.AluOpType

S = 2048
D = 2048
H = 16
KH = 4
HD = 128
W = 512
I = 8192
NTC = S // 128
NIC = D // 128
NOC = I // 512
NJC = I // 128
NDC = D // 128
SCL = 1.0 / np.sqrt(128.0)
C2P23 = 12582912.0
NEG = -1.0e30

ENGINES = ("sync", "gpsimd", "vector", "scalar", "tensor")
SEM_ENGINE = {"v": "vector", "sc": "scalar", "te": "tensor", "gp": "gpsimd"}


class Prog:

    def __init__(self, nc):
        self.nc = nc
        self.sem = {}
        self.counts = {}
        self.ops = {e: [] for e in ENGINES}
        self.last_wait = {e: {} for e in ENGINES}
        self.sems_used = set()

    def phase_barrier(self):
        toks = [(s, self.counts[s]) for s in self.sems_used]
        toks += [(e, self.counts[e]) for e in ("v", "sc", "te", "gp")]
        for e in ENGINES:
            self.op(e, None, waits=toks)
        self.sems_used = set()

    def make_sems(self, stack, names):
        for n in names:
            self.sem[n] = stack.enter_context(self.nc.semaphore(n))
            self.counts[n] = 0

    def op(self, eng, fn=None, waits=(), inc=None, incn=1):
        wl = {}
        for t in waits:
            if t is None:
                continue
            s, v = t
            if v <= 0 or SEM_ENGINE.get(s) == eng:
                continue
            if v > wl.get(s, -1):
                wl[s] = v
        wl2 = []
        for s, v in wl.items():
            if v > self.last_wait[eng].get(s, 0):
                self.last_wait[eng][s] = v
                wl2.append((s, v))
        tok = None
        if inc is not None:
            self.counts[inc] += incn
            tok = (inc, self.counts[inc])
        self.ops[eng].append((fn, wl2, inc, incn))
        return tok

    def dma(self, queue, sem, out, in_, waits=(), transpose=False,
            slow=False):
        semname = sem
        self.sems_used.add(sem)

        def fn(eng, out=out, in_=in_, transpose=transpose, slow=slow):
            if transpose:
                return eng.dma_start_transpose(out=out, in_=in_)
            return eng.dma_start(out=out, in_=in_,
                                 allow_slow_non_contiguous=slow)

        return self.op(queue, fn, waits=waits, inc=semname, incn=16)

    def tok(self, sem):
        return (sem, self.counts[sem])

    def emit(self, block):
        for eng_name in ENGINES:
            ops = self.ops[eng_name]
            if not ops:
                continue

            def make_body(ops=ops):
                def body(eng):
                    for fn, wl, inc, incn in ops:
                        for s, v in wl:
                            eng.wait_ge(self.sem[s], v)
                        if fn is not None:
                            ins = fn(eng)
                            if inc is not None:
                                ins.then_inc(self.sem[inc], incn)
                        elif inc is not None:
                            eng.nop().then_inc(self.sem[inc], incn)
                return body

            getattr(block, eng_name)(make_body())


def bcast_mid(ap2d, n):
    dims = [list(q) for q in ap2d.ap]
    return bass.AP(ap2d.tensor, ap2d.offset, [dims[0], [0, n]] + dims[1:])


def build_program():
    nc = bass.Bass("TRN2", target_bir_lowering=False, debug=False)
    dt = nc.dram_tensor
    x_in = dt("x", [S, D], F32, kind="ExternalInput").ap()
    xT_in = dt("xT", [D, S], F32, kind="ExternalInput").ap()
    lnw_in = dt("lnw", [NIC, 128], F32, kind="ExternalInput").ap()
    wq_in = dt("wq", [D, D], F32, kind="ExternalInput").ap()
    wk_in = dt("wk", [D, KH * HD], F32, kind="ExternalInput").ap()
    wv_in = dt("wv", [D, KH * HD], F32, kind="ExternalInput").ap()
    wo_in = dt("wo", [D, D], F32, kind="ExternalInput").ap()
    wgT_in = dt("wgT", [D, I], F32, kind="ExternalInput").ap()
    wuT_in = dt("wuT", [D, I], F32, kind="ExternalInput").ap()
    wdT_in = dt("wdT", [I, D], F32, kind="ExternalInput").ap()
    mask_in = dt("mask", [128, 640], F32, kind="ExternalInput").ap()
    ropeC_in = dt("ropeC", [S, 64], F32, kind="ExternalInput").ap()
    ropeS_in = dt("ropeS", [S, 64], F32, kind="ExternalInput").ap()
    ident_in = dt("ident", [128, 128], F32, kind="ExternalInput").ap()
    import os
    _K = "ExternalOutput" if os.environ.get("KDBG") else "Internal"
    xq_sp = dt("xq_sp", [NIC * 128, S], BF16, kind=_K).ap()
    q_sp = dt("q_sp", [S, D], BF16, kind=_K).ap()
    h_sp = dt("h_sp", [S, I], F32, kind=_K).ap()
    wdq_sp = dt("wdq_sp", [I, D], BF16, kind=_K).ap()
    part_sp = dt("part_sp", [D, S], F32, kind=_K).ap()
    dbg_sp = dt("dbg_sp", [128, 4 * D + 16], F32, kind=_K).ap()
    dbg_ao = dt("dbg_ao", [128, H * S], BF16, kind=_K).ap()
    dbg_kf = dt("dbg_kf", [128, KH * S], BF16, kind=_K).ap()
    dbg_vt = dt("dbg_vt", [128, NTC * KH * HD], BF16, kind=_K).ap()
    dbg_pt = dt("dbg_pt", [128, H * 5 * 128], BF16, kind=_K).ap()
    dbg_ps = dt("dbg_ps", [128, H * 5 * 128], BF16, kind=_K).ap()
    dbg_wob = dt("dbg_wob", [128, NDC * D], BF16, kind=_K).ap()
    dbg_psw = dt("dbg_psw", [128, 512], F32, kind=_K).ap()
    dbg_hqT = dt("dbg_hqT", [128, NJC * 512], BF16, kind=_K).ap()
    dbg_hsb = dt("dbg_hsb", [128, 512], F32, kind=_K).ap()
    y_out = dt("y", [D, S], F32, kind="ExternalOutput").ap()

    gstack = ExitStack()
    p = Prog(nc)
    DMA_SEMS = ["c0", "f0a", "f0b", "xa", "xb", "na", "nb", "qca", "qcb",
                "xsa", "xsb", "wqa", "wqb", "rpa", "rpb", "qspa", "qspb",
                "wkv", "kfa", "kfb", "qta", "qtb", "qfa", "qfb", "ptx",
                "woa", "wob_", "xfa", "xfb_", "pta", "ptb", "xqr", "wgl",
                "wds", "hsa", "hsb", "hrl", "hqx", "wda", "wdb2", "pfa",
                "pfb2", "yoa", "yob", "dbq"]
    p.make_sems(gstack, ["v", "sc", "te", "gp"] + DMA_SEMS)
    sb = lambda name, shape, dtype=F32: gstack.enter_context(
        nc.sbuf_tensor(name, shape, dtype, side="left"))

    ident = sb("ident_s", [128, 128])
    mask = sb("mask_s", [128, 640])
    lnw_sb = sb("lnw_sb", [128, NIC])
    c_2p23 = sb("c_2p23", [128, 1])
    c_2p23p1 = sb("c_2p23p1", [128, 1])
    c_eps5 = sb("c_eps5", [128, 1])
    c_eps6 = sb("c_eps6", [128, 1])
    ones_col = sb("ones_col", [128, 1])
    ones_row = sb("ones_row", [1, 128])
    ws_vec = sb("ws_vec", [128, 4])
    rws_vec = sb("rws_vec", [128, 4])
    s_all = sb("s_all", [128, NTC])
    negA_g = sb("negA_g", [128, NTC])
    negA_u = sb("negA_u", [128, NTC])
    hs_all = sb("hs_all", [128, 4])
    dump = sb("dump", [128, 4096], BF16)
    dump_sc = sb("dump_sc", [128, D], BF16)
    wsacc = sb("wsacc", [128, 52])

    padv = sb("padv", [128, 1])
    pads = sb("pads", [128, 1])

    nc.const_aps.register(F32, C2P23, c_2p23[:, 0:1])

    def vpad():
        p.op("vector", lambda eng: eng.tensor_copy(padv[:], ones_col[:]))

    def spad():
        p.op("scalar", lambda eng: eng.copy(pads[:], ones_col[:]))

    for t, val in ((c_2p23, C2P23), (c_2p23p1, C2P23 + 1.0), (c_eps5, 1e-5),
                   (c_eps6, 1e-6), (ones_col, 1.0), (ones_row, 1.0)):
        p.op("vector", lambda eng, t=t, val=val: eng.memset(t[:], val),
             inc="v")
    t_conj = p.tok("v")
    t_ident = p.dma("sync", "c0", ident[:], ident_in)
    t_mask = p.dma("sync", "c0", mask[:], mask_in)
    t_lnw = p.dma("sync", "c0", lnw_sb[:], lnw_in.rearrange("a p -> p a"),
                  slow=True)
    t_consts = ("c0", p.counts["c0"])
    t_ident = t_mask = t_lnw = t_consts

    a_era = ExitStack()
    xn2_fm = a_era.enter_context(
        nc.sbuf_tensor("xn2_fm", [128, NIC, S], BF16, side="left"))
    a1 = ExitStack()
    a1sb = lambda name, shape, dtype=F32: a1.enter_context(
        nc.sbuf_tensor(name, shape, dtype, side="right"))
    xbuf = [a1sb(f"xbuf{i}", [128, D]) for i in range(2)]
    xcb = a1sb("xcb", [128, D])
    xnb = a1sb("xnb", [128, D])
    rrb = a1sb("rrb", [128, D])
    xn2c = [a1sb(f"xn2c{i}", [128, D], BF16) for i in range(2)]
    xqc = [a1sb(f"xqc{i}", [128, D], BF16) for i in range(2)]
    stat = a1sb("stat", [128, 8])
    xq_col = [a1sb(f"xqcol{i}", [128, NIC, 128], BF16) for i in range(2)]

    xbuf_tok = [None, None]
    xn2c_tok = [None, None]
    xqc_tok = [None, None]
    xqcol_tok = [None, None]
    t_xn2fm = []
    t_xqsp = []
    for tc in range(NTC):
        bi = tc % 2
        xb = xbuf[bi]
        tl = p.dma("sync", "xa" if bi == 0 else "xb", xb[:],
                   x_in[tc * 128:(tc + 1) * 128, :],
                   waits=[xbuf_tok[bi]])
        p.op("vector",
             lambda eng, xb=xb: eng.tensor_scalar(
                 dump[:, 0:D], xb[:], 1.0, 0.0, ALU.mult, ALU.add,
                 accum_out=stat[:, 0:1]),
             waits=[tl], inc="v")
        vpad()
        p.op("vector",
             lambda eng: eng.tensor_scalar(
                 stat[:, 1:2], stat[:, 0:1], 1.0 / D, None, ALU.mult),
             inc="v")
        vpad()
        t_xc = p.op("vector",
                    lambda eng, xb=xb: eng.tensor_scalar(
                        xcb[:], xb[:], stat[:, 1:2], None, ALU.subtract),
                    inc="v")
        xbuf_tok[bi] = t_xc
        t_vs = p.op("scalar",
                    lambda eng: eng.activation(
                        dump_sc[:, 0:D], xcb[:], AF.Square,
                        accum_out=stat[:, 2:3]),
                    waits=[t_xc], inc="sc")
        spad()
        t_sd = p.op("scalar",
                    lambda eng: eng.activation(
                        stat[:, 3:4], stat[:, 2:3], AF.Sqrt,
                        bias=c_eps5[:, 0:1], scale=1.0 / D),
                    waits=[t_conj], inc="sc")
        t_rstd = p.op("vector",
                      lambda eng: eng.reciprocal(stat[:, 4:5], stat[:, 3:4]),
                      waits=[t_sd], inc="v")
        vpad()
        t_xn = p.op("vector",
                    lambda eng: eng.tensor_scalar(
                        xnb[:], xcb[:], stat[:, 4:5], None, ALU.mult),
                    inc="v")
        t_r2s = p.op("scalar",
                     lambda eng: eng.activation(
                         dump_sc[:, 0:D], xnb[:], AF.Square,
                         accum_out=stat[:, 2:3]),
                     waits=[t_xn], inc="sc")
        spad()
        t_sd2 = p.op("scalar",
                     lambda eng: eng.activation(
                         stat[:, 3:4], stat[:, 2:3], AF.Sqrt,
                         bias=c_eps6[:, 0:1], scale=1.0 / D),
                     inc="sc")
        t_r2 = p.op("vector",
                    lambda eng: eng.reciprocal(stat[:, 4:5], stat[:, 3:4]),
                    waits=[t_sd2], inc="v")
        vpad()
        t_xn2 = p.op("vector",
                     lambda eng, o=xn2c[bi]: eng.tensor_scalar(
                         o[:], xnb[:], stat[:, 4:5], None, ALU.mult),
                     waits=[xn2c_tok[bi]], inc="v")
        p.op("vector",
             lambda eng: eng.reduce_max(
                 stat[:, 5:6], xnb[:], axis=mybir.AxisListType.X,
                 apply_absolute_value=True),
             inc="v")
        vpad()
        p.op("vector",
             lambda eng, tc=tc: eng.tensor_scalar(
                 s_all[:, tc:tc + 1], stat[:, 5:6], 1.0 / 127.0, 1e-5,
                 ALU.mult, ALU.add),
             inc="v")
        vpad()
        t_rs = p.op("vector",
                    lambda eng, tc=tc: eng.reciprocal(
                        stat[:, 7:8], s_all[:, tc:tc + 1]),
                    inc="v")
        t_rr = p.op("scalar",
                    lambda eng: eng.activation(
                        rrb[:], xnb[:], AF.Copy, bias=C2P23,
                        scale=stat[:, 7:8]),
                    waits=[t_rs, t_conj], inc="sc")
        t_xq = p.op("vector",
                    lambda eng, o=xqc[bi]: eng.tensor_scalar(
                        o[:], rrb[:], C2P23, None, ALU.subtract),
                    waits=[t_rr, xqc_tok[bi]], inc="v")
        if tc == 0 and os.environ.get("KDBG"):
            dts = [p.dma("sync", "dbq", dbg_sp[:, 0:D], xbuf[bi][:], waits=[t_xq]),
                   p.dma("sync", "dbq", dbg_sp[:, D:2 * D], xcb[:], waits=[t_xq]),
                   p.dma("sync", "dbq", dbg_sp[:, 2 * D:3 * D], xnb[:],
                         waits=[t_xq]),
                   p.dma("sync", "dbq", dbg_sp[:, 3 * D:4 * D], rrb[:],
                         waits=[t_xq]),
                   p.dma("sync", "dbq", dbg_sp[:, 4 * D:4 * D + 8], stat[:, 0:8],
                         waits=[t_xq])]
            p.op("vector", None, waits=dts)
            p.op("scalar", None, waits=dts)
        t_x1 = p.dma("sync", "na" if bi == 0 else "nb",
                     xn2_fm[:, :, tc * 128:(tc + 1) * 128],
                     xn2c[bi].ap().rearrange("p (a b) -> p a b", b=128),
                     waits=[t_xn2], transpose=True)
        xn2c_tok[bi] = t_x1
        t_x2 = p.dma("sync", "qca" if bi == 0 else "qcb", xq_col[bi][:],
                     xqc[bi].ap().rearrange("p (a b) -> p a b", b=128),
                     waits=[t_xq, xqcol_tok[bi]], transpose=True)
        xqc_tok[bi] = t_x2
        t_sp = p.dma("gpsimd", "xsa" if bi == 0 else "xsb",
                     xq_sp.rearrange("(a p) t -> p a t", p=128)[
                         :, :, tc * 128:(tc + 1) * 128],
                     xq_col[bi][:], waits=[t_x2])
        xqcol_tok[bi] = t_sp
        t_xn2fm.append(t_x1)
        t_xqsp.append(t_sp)
    t_xqsp_tot = [("xsa", p.counts["xsa"]), ("xsb", p.counts["xsb"])]
    a1.close()
    p.phase_barrier()

    f0 = ExitStack()
    wbuf = [f0.enter_context(nc.sbuf_tensor(f"f0w{i}", [128, I], F32,
                                                side="right"))
            for i in range(2)]
    f0dump = f0.enter_context(nc.sbuf_tensor("f0dump", [128, I], BF16,
                                             side="right"))
    ps_ws = f0.enter_context(nc.psum_tensor("ps_ws", [128, 4], F32))
    ws_row = f0.enter_context(nc.sbuf_tensor("ws_row", [1, 4], F32,
                                                 side="right"))
    rws_row = f0.enter_context(nc.sbuf_tensor("rws_row", [1, 4], F32,
                                              side="right"))
    srcs = [
        (wgT_in.rearrange("(a p) c -> p a c", p=128), NIC, 0),
        (wuT_in.rearrange("(a p) c -> p a c", p=128), NIC, 16),
        (wdT_in.rearrange("(c p) d -> p c d", p=128), 16, 32),
    ]
    buf_tok = [None, None]
    t_sum_toks = []
    idx = 0
    for mi, (src, nch, base) in enumerate(srcs):
        for c in range(nch):
            bi = idx % 2
            if mi < 2:
                chunk = src[:, c, :]
                dst = wbuf[bi][:]
            else:
                chunk = src[:, c * 4:(c + 1) * 4, :]
                dst = wbuf[bi].ap().rearrange("p (a b) -> p a b", b=2048)
            tl = p.dma("gpsimd", "f0a" if bi == 0 else "f0b", dst, chunk,
                       waits=[buf_tok[bi]])
            buf_tok[bi] = p.op(
                "scalar",
                lambda eng, b=wbuf[bi], k=base + c: eng.activation(
                    f0dump[:], b[:], AF.Abs,
                    accum_out=wsacc[:, k:k + 1]),
                waits=[tl], inc="sc")
            idx += 1
    t_red = []
    for mi, (lo, hi) in enumerate(((0, 16), (16, 32), (32, 48))):
        t_red.append(p.op(
            "vector",
            lambda eng, mi=mi, lo=lo, hi=hi: eng.reduce_sum(
                wsacc[:, 48 + mi:49 + mi], wsacc[:, lo:hi],
                axis=mybir.AxisListType.X),
            waits=[buf_tok[0], buf_tok[1]], inc="v"))
    t_mm_ws = []
    for mi in range(3):
        t_mm_ws.append(p.op(
            "tensor",
            lambda eng, mi=mi: eng.matmul(
                ps_ws[0:1, mi:mi + 1], ones_col[:, 0:1],
                wsacc[:, 48 + mi:49 + mi], start=True, stop=True),
            waits=[t_red[mi], t_conj], inc="te"))
    t_wsr = p.op("vector",
                 lambda eng: eng.tensor_scalar(
                     ws_row[0:1, 0:3], ps_ws[0:1, 0:3], 1.0 / 16777216.0,
                     1e-5, ALU.mult, ALU.add),
                 waits=t_mm_ws, inc="v")
    vpad()
    t_rwsr = p.op("vector",
                  lambda eng: eng.reciprocal(rws_row[0:1, 0:3],
                                             ws_row[0:1, 0:3]),
                  waits=[t_wsr], inc="v")
    t_bc1 = p.op("tensor",
                 lambda eng: eng.matmul(ps_ws[:, 0:3], ones_row[0:1, :],
                                        ws_row[0:1, 0:3], start=True,
                                        stop=True),
                 waits=[t_rwsr, t_conj], inc="te")
    t_wsb = p.op("vector",
                 lambda eng: eng.tensor_copy(ws_vec[:, 0:3], ps_ws[:, 0:3]),
                 waits=[t_bc1], inc="v")
    t_bc2 = p.op("tensor",
                 lambda eng: eng.matmul(ps_ws[:, 0:3], ones_row[0:1, :],
                                        rws_row[0:1, 0:3], start=True,
                                        stop=True),
                 waits=[t_wsb], inc="te")
    t_rwsb = p.op("vector",
                  lambda eng: eng.tensor_copy(rws_vec[:, 0:3],
                                              ps_ws[:, 0:3]),
                  waits=[t_bc2], inc="v")

    t_nA = []
    t_nA.append(p.op("vector",
                     lambda eng: eng.tensor_scalar(
                         negA_g[:], s_all[:], ws_vec[:, 0:1], -1.0, ALU.mult,
                         ALU.mult),
                     waits=[t_wsb], inc="v"))
    t_nA.append(p.op("vector",
                     lambda eng: eng.tensor_scalar(
                         negA_u[:], s_all[:], ws_vec[:, 1:2], -1.0, ALU.mult,
                         ALU.mult),
                     inc="v"))
    f0.close()

    p.phase_barrier()

    a2 = ExitStack()
    a2sb = lambda name, shape, dtype=F32: a2.enter_context(
        nc.sbuf_tensor(name, shape, dtype, side="right"))
    wqf = [a2sb(f"wqf{i}", [128, NIC, 512]) for i in range(2)]
    wqb = a2sb("wqb", [128, NIC, 512], BF16)
    ropeC = [a2sb(f"ropeC{i}", [128, 64]) for i in range(2)]
    ropeS = [a2sb(f"ropeS{i}", [128, 64]) for i in range(2)]
    rope_sc = [a2sb(f"ropesc{i}", [128, 256]) for i in range(4)]
    qrot = [a2sb(f"qrot{i}", [128, 512], BF16) for i in range(2)]
    ps_q = [a2.enter_context(nc.psum_tensor(f"ps_q{i}", [128, 512], F32))
            for i in range(2)]

    def emit_rope(ps, nheads, Cb, Sb, scr, out_bf, waits):
        ve = ps.ap().rearrange("p (h c) -> p h c", c=128)[:, :, 0:64]
        vo = ps.ap().rearrange("p (h c) -> p h c", c=128)[:, :, 64:128]
        Cb3 = bcast_mid(Cb, nheads)
        Sb3 = bcast_mid(Sb, nheads)
        vws = [scr[i].ap().rearrange("p (h c) -> p h c", c=64)[:, 0:nheads]
               for i in range(4)]
        oe = out_bf.ap().rearrange("p (h c) -> p h c", c=128)[:, :, 0:64]
        oo = out_bf.ap().rearrange("p (h c) -> p h c", c=128)[:, :, 64:128]
        p.op("vector", lambda eng: eng.tensor_tensor(vws[0], ve, Cb3,
                                                     ALU.mult),
             waits=waits, inc="v")
        p.op("vector", lambda eng: eng.tensor_tensor(vws[1], vo, Sb3,
                                                     ALU.mult), inc="v")
        p.op("vector", lambda eng: eng.tensor_tensor(vws[2], ve, Sb3,
                                                     ALU.mult), inc="v")
        p.op("vector", lambda eng: eng.tensor_tensor(vws[3], vo, Cb3,
                                                     ALU.mult), inc="v")
        p.op("vector", lambda eng: eng.tensor_tensor(oe, vws[0], vws[1],
                                                     ALU.subtract), inc="v")
        return p.op("vector", lambda eng: eng.tensor_tensor(oo, vws[2],
                                                            vws[3], ALU.add),
                    inc="v")

    wq_src = wq_in.rearrange("(a p) o -> p a o", p=128)
    ropeb_tok = [None, None]
    qrot_tok = [None, None]
    psq_tok = [None, None]
    wqf_tok = [None, None]
    t_last_qmm = None
    t_qsp = [None] * NTC
    for qo in range(4):
        wi = qo % 2
        twl = p.dma("gpsimd", "wqa" if wi == 0 else "wqb", wqf[wi][:],
                    wq_src[:, :, qo * 512:(qo + 1) * 512],
                    waits=[wqf_tok[wi]])
        casts = []
        for ic in range(NIC):
            casts.append(p.op(
                "vector",
                lambda eng, ic=ic, wi=wi: eng.tensor_scalar(
                    wqb[:, ic, :], wqf[wi][:, ic, :], lnw_sb[:, ic:ic + 1],
                    None, ALU.mult),
                waits=[twl, t_lnw, t_last_qmm, t_nA[-1]], inc="v"))
        wqf_tok[wi] = casts[-1]
        t_cast = casts[-1]
        for tc in range(NTC):
            ri = tc % 2
            pi = tc % 2
            rsem = "rpa" if ri == 0 else "rpb"
            tC = p.dma("sync", rsem, ropeC[ri][:],
                       ropeC_in[tc * 128:(tc + 1) * 128, :],
                       waits=[ropeb_tok[ri]])
            tS = p.dma("sync", rsem, ropeS[ri][:],
                       ropeS_in[tc * 128:(tc + 1) * 128, :])
            mm_last = None
            for ic in range(NIC):
                mm_last = p.op(
                    "tensor",
                    lambda eng, ic=ic, tc=tc, pi=pi: eng.matmul(
                        ps_q[pi][:], xn2_fm[:, ic, tc * 128:(tc + 1) * 128],
                        wqb[:, ic, :], start=(ic == 0), stop=(ic == NIC - 1)),
                    waits=([t_cast, psq_tok[pi], t_xn2fm[tc]]
                           if ic == 0 else ()),
                    inc="te")
            t_last_qmm = mm_last
            t_rope = emit_rope(ps_q[pi], 4, ropeC[ri][:], ropeS[ri][:],
                               rope_sc, qrot[pi],
                               [mm_last, tC, tS, qrot_tok[pi]])
            ropeb_tok[ri] = t_rope
            psq_tok[pi] = t_rope
            tqs = p.dma("gpsimd", "qspa" if pi == 0 else "qspb",
                        q_sp[tc * 128:(tc + 1) * 128,
                             qo * 512:(qo + 1) * 512],
                        qrot[pi][:], waits=[t_rope])
            qrot_tok[pi] = tqs
            t_qsp[tc] = tqs
    a2.close()

    p.phase_barrier()

    a3 = ExitStack()
    a3sb = lambda name, shape, dtype=F32: a3.enter_context(
        nc.sbuf_tensor(name, shape, dtype, side="right"))
    kv_era = ExitStack()
    k_fm = kv_era.enter_context(
        nc.sbuf_tensor("k_fm", [128, KH, S], BF16, side="right"))
    v_tm = kv_era.enter_context(
        nc.sbuf_tensor("v_tm", [128, NTC, KH * HD], BF16, side="right"))
    wkf = a3sb("wkf", [128, NIC, 512])
    wkb = a3sb("wkb", [128, NIC, 512], BF16)
    wvb = a3sb("wvb", [128, NIC, 512], BF16)
    ropeCk = [a3sb(f"ropeCk{i}", [128, 64]) for i in range(2)]
    ropeSk = [a3sb(f"ropeSk{i}", [128, 64]) for i in range(2)]
    rope_sck = [a3sb(f"ropesck{i}", [128, 256]) for i in range(4)]
    krot = [a3sb(f"krot{i}", [128, 512], BF16) for i in range(2)]
    ps_k = [a3.enter_context(nc.psum_tensor(f"ps_k{i}", [128, 512], F32))
            for i in range(2)]
    ps_v = [a3.enter_context(nc.psum_tensor(f"ps_v{i}", [128, 512], F32))
            for i in range(2)]

    twk = p.dma("gpsimd", "wkv", wkf[:], wk_in.rearrange("(a p) o -> p a o", p=128))
    ckl = None
    for ic in range(NIC):
        ckl = p.op(
            "vector",
            lambda eng, ic=ic: eng.tensor_scalar(
                wkb[:, ic, :], wkf[:, ic, :], lnw_sb[:, ic:ic + 1], None,
                ALU.mult),
            waits=[twk], inc="v")
    twv = p.dma("gpsimd", "wkv", wkf[:], wv_in.rearrange("(a p) o -> p a o", p=128),
                waits=[ckl])
    for ic in range(NIC):
        ckl = p.op(
            "vector",
            lambda eng, ic=ic: eng.tensor_scalar(
                wvb[:, ic, :], wkf[:, ic, :], lnw_sb[:, ic:ic + 1], None,
                ALU.mult),
            waits=[twv], inc="v")
    t_wkvb = ckl

    ropek_tok = [None, None]
    krot_tok = [None, None]
    psk_tok = [None, None]
    psv_tok = [None, None]
    kfm_tok = [None, None]
    t_kx_l = [None] * NTC
    t_vev_l = [None] * NTC
    for tc in range(NTC):
        ri = tc % 2
        pi = tc % 2
        rsem = "rpa" if ri == 0 else "rpb"
        tC = p.dma("sync", rsem, ropeCk[ri][:],
                   ropeC_in[tc * 128:(tc + 1) * 128, :],
                   waits=[ropek_tok[ri]])
        tS = p.dma("sync", rsem, ropeSk[ri][:],
                   ropeS_in[tc * 128:(tc + 1) * 128, :])
        mmk_last = None
        mmv_last = None
        for ic in range(NIC):
            mmk_last = p.op(
                "tensor",
                lambda eng, ic=ic, tc=tc, pi=pi: eng.matmul(
                    ps_k[pi][:], xn2_fm[:, ic, tc * 128:(tc + 1) * 128],
                    wkb[:, ic, :], start=(ic == 0), stop=(ic == NIC - 1)),
                waits=([t_wkvb, psk_tok[pi]] if ic == 0 else ()),
                inc="te")
            mmv_last = p.op(
                "tensor",
                lambda eng, ic=ic, tc=tc, pi=pi: eng.matmul(
                    ps_v[pi][:], xn2_fm[:, ic, tc * 128:(tc + 1) * 128],
                    wvb[:, ic, :], start=(ic == 0), stop=(ic == NIC - 1)),
                waits=([psv_tok[pi]] if ic == 0 else ()),
                inc="te")
        t_ropek = emit_rope(ps_k[pi], KH, ropeCk[ri][:], ropeSk[ri][:],
                            rope_sck, krot[pi],
                            [mmk_last, tC, tS, krot_tok[pi]])
        ropek_tok[ri] = t_ropek
        psk_tok[pi] = t_ropek
        t_vev = p.op("vector",
                     lambda eng, tc=tc, pi=pi: eng.tensor_copy(
                         v_tm[:, tc, :], ps_v[pi][:]),
                     waits=[mmv_last], inc="v")
        psv_tok[pi] = t_vev
        t_kx = p.dma("sync", "kfa" if pi == 0 else "kfb",
                     k_fm[:, :, tc * 128:(tc + 1) * 128],
                     krot[pi].ap().rearrange("p (a b) -> p a b", b=128),
                     waits=[t_ropek, kfm_tok[pi]], transpose=True)
        kfm_tok[pi] = t_kx
        krot_tok[pi] = t_kx
        t_kx_l[tc] = t_kx
        t_vev_l[tc] = t_vev
    t_a3_vfence = p.tok("v")
    a3.close()
    a_era.close()

    p.phase_barrier()

    b_era = ExitStack()
    attn_out = b_era.enter_context(
        nc.sbuf_tensor("attn_out", [128, H, S], BF16, side="left"))
    bph = ExitStack()
    bsb = lambda name, shape, dtype=F32: bph.enter_context(
        nc.sbuf_tensor(name, shape, dtype, side="right"))
    qc_tm = [bsb(f"qc_tm{i}", [128, D], BF16) for i in range(2)]
    qc_fm = [bsb(f"qc_fm{i}", [128, H, 128], BF16) for i in range(2)]
    sc_f32 = [bsb(f"sc_f32{i}", [128, 640]) for i in range(2)]
    neg_sc = [bsb(f"neg_sc{i}", [128, 640]) for i in range(2)]
    expb = [bsb(f"expb{i}", [128, 640]) for i in range(2)]
    smstat = bsb("smstat", [128, 4])
    probs = [bsb("probs0", [128, H, 5, 128], BF16)] * 2
    probsT = [bsb("probsT0", [128, H, 5, 128], BF16)] * 2
    ps_s = [bph.enter_context(nc.psum_tensor(f"ps_s{i}", [128, 1024], F32))
            for i in range(2)]
    ps_av = bph.enter_context(nc.psum_tensor("ps_av", [128, 2048], F32))

    pm = p.op("vector", lambda eng: eng.memset(
        probs[0].ap().rearrange("p a b c -> p (a b c)"), 0.0), inc="v")
    qctm_tok = [None, None]
    qcfm_tok = [None, None]
    probs_tok = [pm, pm]
    probsT_tok = [None, None]
    ps_s_tok = [None, None]
    ps_av_tok = None
    t_b_vlast = None
    for tc in range(NTC):
        bi = tc % 2
        nkc = min(tc + 1, 5)
        kc0 = tc - nkc + 1
        wcols = nkc * 128
        c0 = kc0 * 128
        moff = 640 - wcols
        tq = p.dma("sync", "qta" if bi == 0 else "qtb", qc_tm[bi][:],
                   q_sp[tc * 128:(tc + 1) * 128, :],
                   waits=[qctm_tok[bi], t_qsp[tc]])
        tqx = p.dma("sync", "qfa" if bi == 0 else "qfb", qc_fm[bi][:],
                    qc_tm[bi].ap().rearrange("p (a b) -> p a b", b=128),
                    waits=[tq, qcfm_tok[bi]], transpose=True)
        qctm_tok[bi] = tqx
        ev_toks = []

        def finalize(h, t_ex):
            hp = h % 2
            t_rd = p.op("vector",
                        lambda eng, hp=hp: eng.reciprocal(
                            smstat[:, 2 + hp:3 + hp],
                            smstat[:, 2 + hp:3 + hp]),
                        waits=[t_ex], inc="v")
            vpad()
            t_pr = p.op(
                "vector",
                lambda eng, h=h, hp=hp, bi=bi, nkc=nkc, wcols=wcols:
                eng.tensor_scalar(
                    probs[bi][:, h, 5 - nkc:5, :].rearrange(
                        "p a b -> p (a b)"),
                    expb[hp][:, 0:wcols], smstat[:, 2 + hp:3 + hp], None,
                    ALU.mult),
                waits=[probs_tok[bi]], inc="v")
            ev_toks.append(t_pr)

        prev = None
        for h in range(H):
            si = h % 2
            hp = h % 2
            kh = h // 4
            t_mm = []
            n_done = 0
            while n_done < wcols:
                n = min(512, wcols - n_done)
                first = (n_done == 0)
                t_mm.append(p.op(
                    "tensor",
                    lambda eng, h=h, kh=kh, si=si, n=n, nd=n_done, c0=c0,
                    bi=bi: eng.matmul(
                        ps_s[si][:, nd:nd + n], qc_fm[bi][:, h, :],
                        k_fm[:, kh, c0 + nd:c0 + nd + n],
                        start=True, stop=True),
                    waits=([tqx, ps_s_tok[si], t_kx_l[tc], t_vev_l[tc],
                            t_a3_vfence] if first else ()),
                    inc="te"))
                n_done += n
            t_ev = p.op(
                "vector",
                lambda eng, wcols=wcols, moff=moff, si=si, hp=hp:
                eng.tensor_tensor(
                    sc_f32[hp][:, 0:wcols], ps_s[si][:, 0:wcols],
                    mask[:, moff:moff + wcols], ALU.add),
                waits=t_mm + [t_mask], inc="v")
            ps_s_tok[si] = t_ev
            t_ng = p.op(
                "vector",
                lambda eng, wcols=wcols, hp=hp: eng.tensor_scalar(
                    neg_sc[hp][:, 0:wcols], sc_f32[hp][:, 0:wcols], -SCL,
                    3.0e38, ALU.mult, ALU.min,
                    accum_out=smstat[:, hp:hp + 1]),
                inc="v")
            t_ex = p.op(
                "scalar",
                lambda eng, wcols=wcols, hp=hp: eng.activation(
                    expb[hp][:, 0:wcols], neg_sc[hp][:, 0:wcols], AF.Exp,
                    bias=smstat[:, hp:hp + 1], scale=-1.0,
                    accum_out=smstat[:, 2 + hp:3 + hp]),
                waits=[t_ng], inc="sc")
            if prev is not None:
                finalize(*prev)
            prev = (h, t_ex)
        finalize(*prev)
        probs_tok[0] = probs_tok[1] = ev_toks[-1]
        qcfm_tok[bi] = t_mm[-1]
        t_px = p.dma("sync", "ptx", probsT[bi][:], probs[bi][:],
                     waits=ev_toks + [probsT_tok[bi]], transpose=True)
        probs_tok[0] = probs_tok[1] = t_px
        if os.environ.get("KDBG") and tc == int(os.environ.get("DBGTC", 15)):
            d1 = p.dma("sync", "dbq", dbg_ps,
                       probs[bi].ap().rearrange("p a b c -> p (a b c)"),
                       waits=[t_px])
            d2 = p.dma("sync", "dbq", dbg_pt,
                       probsT[bi].ap().rearrange("p a b c -> p (a b c)"),
                       waits=[t_px])
            p.op("vector", None, waits=[d1, d2])
            p.op("tensor", None, waits=[d1, d2])
        t_avmm = None
        for h in range(H):
            kh = h // 4
            for i_kc in range(nkc):
                kci = 5 - nkc + i_kc
                kc_abs = kc0 + i_kc
                first = (h == 0 and i_kc == 0)
                t_avmm = p.op(
                    "tensor",
                    lambda eng, h=h, kh=kh, kci=kci, kc_abs=kc_abs, bi=bi,
                    i_kc=i_kc, nkc=nkc: eng.matmul(
                        ps_av[:, h * 128:(h + 1) * 128],
                        v_tm[:, kc_abs, kh * 128:(kh + 1) * 128],
                        probsT[bi][:, h, kci, :],
                        start=(i_kc == 0), stop=(i_kc == nkc - 1)),
                    waits=([t_px, ps_av_tok] if first else ()),
                    inc="te")
        probsT_tok[0] = probsT_tok[1] = t_avmm
        evs = []
        for g in range(4):
            evs.append(p.op(
                "vector",
                lambda eng, g=g, tc=tc: eng.tensor_copy(
                    attn_out[:, 4 * g:4 * (g + 1), tc * 128:(tc + 1) * 128],
                    ps_av[:, g * 512:(g + 1) * 512].rearrange(
                        "p (a b) -> p a b", b=128)),
                waits=[t_avmm], inc="v"))
        ps_av_tok = evs[-1]
        t_b_vlast = evs[-1]
    if os.environ.get("KDBG"):
        p.dma("sync", "dbq", dbg_ao, attn_out.ap().rearrange("p a b -> p (a b)"),
              waits=[t_b_vlast])
        p.dma("sync", "dbq", dbg_kf, k_fm.ap().rearrange("p a b -> p (a b)"),
              waits=[t_b_vlast])
        p.dma("sync", "dbq", dbg_vt, v_tm.ap().rearrange("p a b -> p (a b)"),
              waits=[t_b_vlast])
    bph.close()
    kv_era.close()

    p.phase_barrier()

    wop = ExitStack()
    wsb_ = lambda name, shape, dtype=F32: wop.enter_context(
        nc.sbuf_tensor(name, shape, dtype, side="right"))
    wof = [wsb_(f"wof{i}", [128, D]) for i in range(2)]
    wob = wsb_("wob", [128, NDC, D], BF16)
    xfb = [wsb_(f"xfb{i}", [128, 512]) for i in range(2)]
    pblk = [wsb_(f"pblk{i}", [128, 512]) for i in range(2)]
    ps_wo = [wop.enter_context(nc.psum_tensor(f"ps_wo{i}", [128, 512], F32))
             for i in range(2)]
    wof_tok = [None, None]
    t_wob = None
    for ic in range(NIC):
        wi = ic % 2
        twl = p.dma("gpsimd", "woa" if wi == 0 else "wob_", wof[wi][:],
                    wo_in[ic * 128:(ic + 1) * 128, :], waits=[wof_tok[wi]])
        t_wob = p.op("vector",
                     lambda eng, ic=ic, wi=wi: eng.tensor_copy(
                         wob[:, ic, :], wof[wi][:]),
                     waits=[twl], inc="v")
        wof_tok[wi] = t_wob
    xfb_tok = [None, None]
    pblk_tok = [None, None]
    ps_wo_tok = [None, None]
    t_part = [[None] * 4 for _ in range(NDC)]
    for dc in range(NDC):
        for tt in range(4):
            pi = (dc * 4 + tt) % 2
            mm_last = None
            for h in range(H):
                mm_last = p.op(
                    "tensor",
                    lambda eng, h=h, dc=dc, tt=tt, pi=pi: eng.matmul(
                        ps_wo[pi][:], wob[:, h, dc * 128:(dc + 1) * 128],
                        attn_out[:, h, tt * 512:(tt + 1) * 512],
                        start=(h == 0), stop=(h == H - 1)),
                    waits=([t_wob, ps_wo_tok[pi], t_b_vlast]
                           if h == 0 else ()),
                    inc="te")
            if os.environ.get("KDBG2") and dc == 0 and tt == 0:
                dw = p.dma("sync", "dbq", dbg_wob,
                           wob.ap().rearrange("p a b -> p (a b)"),
                           waits=[t_wob])
                for e in ("tensor", "vector", "scalar", "gpsimd"):
                    p.op(e, None, waits=[dw])
            tx = p.dma("sync", "xfa" if pi == 0 else "xfb_", xfb[pi][:],
                       xT_in[dc * 128:(dc + 1) * 128,
                             tt * 512:(tt + 1) * 512],
                       waits=[xfb_tok[pi]])
            t_add = p.op(
                "vector",
                lambda eng, pi=pi: eng.tensor_tensor(
                    pblk[pi][:], ps_wo[pi][:], xfb[pi][:], ALU.add),
                waits=[mm_last, tx, pblk_tok[pi]], inc="v")
            ps_wo_tok[pi] = t_add
            xfb_tok[pi] = t_add
            t_st = p.dma("sync", "pta" if pi == 0 else "ptb",
                         part_sp[dc * 128:(dc + 1) * 128,
                                 tt * 512:(tt + 1) * 512],
                         pblk[pi][:], waits=[t_add])
            pblk_tok[pi] = t_st
            t_part[dc][tt] = t_st
    t_wo_vfence = p.tok("v")
    wop.close()
    b_era.close()

    p.phase_barrier()

    f1 = ExitStack()
    fsb = lambda name, shape, dtype=F32: f1.enter_context(
        nc.sbuf_tensor(name, shape, dtype, side="right"))
    xq_rd = fsb("xq_rd", [128, NIC, S], BF16)
    wgf = fsb("wgf", [128, NIC, 512])
    wgq = [fsb(f"wgq{i}", [128, NIC, 512], BF16) for i in range(2)]
    wuq = [fsb(f"wuq{i}", [128, NIC, 512], BF16) for i in range(2)]
    wdq = fsb("wdq", [128, 1, D], BF16)
    tq1 = fsb("tq1", [128, NIC * 256])
    sgb = [fsb(f"sgb{i}", [128, 512]) for i in range(2)]
    hbl = [fsb(f"hbl{i}", [128, 512]) for i in range(2)]
    ps_g = [f1.enter_context(nc.psum_tensor(f"ps_g{i}", [128, 512], F32))
            for i in range(2)]
    ps_u = [f1.enter_context(nc.psum_tensor(f"ps_u{i}", [128, 512], F32))
            for i in range(2)]

    t_xqrd = p.dma("gpsimd", "xqr", xq_rd[:],
                   xq_sp.rearrange("(a p) t -> p a t", p=128),
                   waits=t_xqsp_tot)

    tq1_tok = [None]

    def emit_quant(src_f32, dst_bf, rws_slice, waits, nfree):
        t1 = p.op("scalar",
                  lambda eng, src=src_f32, nf=nfree, r=rws_slice:
                  eng.activation(tq1[:, 0:nf], src, AF.Copy, bias=C2P23,
                                 scale=r),
                  waits=list(waits) + [t_conj, t_rwsb, tq1_tok[0]],
                  inc="sc")
        t2 = p.op("scalar",
                  lambda eng, nf=nfree: eng.activation(
                      tq1[:, 0:nf], tq1[:, 0:nf], AF.Relu,
                      bias=c_2p23p1[:, 0:1], scale=-1.0),
                  inc="sc")
        t3 = p.op("vector",
                  lambda eng, dst=dst_bf, nf=nfree: eng.tensor_scalar(
                      dst, tq1[:, 0:nf], 2.0, 1.0, ALU.min, ALU.subtract),
                  waits=[t2], inc="v")
        tq1_tok[0] = t3
        return t3

    h_dst = h_sp.rearrange("(a p) o -> p a o", p=128)
    wgT_src = wgT_in.rearrange("(a p) c -> p a c", p=128)
    wuT_src = wuT_in.rearrange("(a p) c -> p a c", p=128)
    wdT_src = wdT_in.rearrange("(a p) c -> p a c", p=128)
    wdq_dst = wdq_sp.rearrange("(a p) c -> p a c", p=128)
    wgq_tok = [None, None]
    wuq_tok = [None, None]
    wdq_tok = [None]
    sgb_tok = [None, None]
    hbl_tok = [None, None]
    ps_g_tok = [None, None]
    ps_u_tok = [None, None]
    t_h_last = None
    t_wdq_last = None
    def emit_wq(ot):
        oi = ot % 2
        tg = p.dma("gpsimd", "wgl", wgf[:],
                   wgT_src[:, :, ot * 512:(ot + 1) * 512],
                   waits=[tq1_tok[0]])
        for hv2 in range(2):
            tgq = emit_quant(
                wgf.ap()[:, hv2 * 8:(hv2 + 1) * 8, :].rearrange(
                    "p a c -> p (a c)"),
                wgq[oi].ap()[:, hv2 * 8:(hv2 + 1) * 8, :].rearrange(
                    "p a c -> p (a c)"),
                rws_vec[:, 0:1], [tg, wgq_tok[oi]], NIC * 256)
        tu = p.dma("gpsimd", "wgl", wgf[:],
                   wuT_src[:, :, ot * 512:(ot + 1) * 512],
                   waits=[tgq])
        for hv2 in range(2):
            tuq = emit_quant(
                wgf.ap()[:, hv2 * 8:(hv2 + 1) * 8, :].rearrange(
                    "p a c -> p (a c)"),
                wuq[oi].ap()[:, hv2 * 8:(hv2 + 1) * 8, :].rearrange(
                    "p a c -> p (a c)"),
                rws_vec[:, 1:2], [tu, wuq_tok[oi]], NIC * 256)
        wgf4 = wgf.ap().rearrange("p a c -> p (a c)").rearrange(
            "p (a c) -> p a c", c=D)
        td = p.dma("gpsimd", "wgl", wgf4,
                   wdT_src[:, ot * 4:(ot + 1) * 4, :],
                   waits=[tq1_tok[0], tuq])
        for hv in range(4):
            tdq = emit_quant(
                wgf4[:, hv:hv + 1, :].rearrange("p a c -> p (a c)"),
                wdq.ap().rearrange("p a c -> p (a c)"),
                rws_vec[:, 2:3], [td, wdq_tok[0]], D)
            tds = p.dma("sync", "wds",
                        wdq_dst[:, ot * 4 + hv:ot * 4 + hv + 1, :],
                        wdq[:], waits=[tdq])
            wdq_tok[0] = tds
            tok_wdq[0] = tds
        return tgq, tuq

    tok_wdq = [None]
    wq_toks = {0: emit_wq(0)}
    for ot in range(NOC):
        oi = ot % 2
        if ot + 1 < NOC:
            wq_toks[ot + 1] = emit_wq(ot + 1)
        tgq, tuq = wq_toks[ot]
        t_wdq_last = tok_wdq[0]
        mg_last = None
        mu_last = None
        for tc in range(NTC):
            pi = tc % 2
            for ic in range(NIC):
                mg_last = p.op(
                    "tensor",
                    lambda eng, ic=ic, tc=tc, pi=pi, oi=oi: eng.matmul(
                        ps_g[pi][:], xq_rd[:, ic, tc * 128:(tc + 1) * 128],
                        wgq[oi][:, ic, :], start=(ic == 0),
                        stop=(ic == NIC - 1)),
                    waits=([tgq, t_xqrd, ps_g_tok[pi]] if ic == 0 else ()),
                    inc="te")
                mu_last = p.op(
                    "tensor",
                    lambda eng, ic=ic, tc=tc, pi=pi, oi=oi: eng.matmul(
                        ps_u[pi][:], xq_rd[:, ic, tc * 128:(tc + 1) * 128],
                        wuq[oi][:, ic, :], start=(ic == 0),
                        stop=(ic == NIC - 1)),
                    waits=([tuq, ps_u_tok[pi]] if ic == 0 else ()),
                    inc="te")
            t_sg = p.op(
                "scalar",
                lambda eng, tc=tc, pi=pi: eng.activation(
                    sgb[pi][:], ps_g[pi][:], AF.Silu,
                    scale=negA_g[:, tc:tc + 1]),
                waits=[mg_last, t_nA[0], sgb_tok[pi]], inc="sc")
            ps_g_tok[pi] = t_sg
            t_h = p.op(
                "vector",
                lambda eng, tc=tc, pi=pi: eng.scalar_tensor_tensor(
                    hbl[pi][:], ps_u[pi][:], negA_u[:, tc:tc + 1], sgb[pi][:],
                    ALU.mult, ALU.mult),
                waits=[mu_last, t_sg, t_nA[1], hbl_tok[pi]], inc="v")
            ps_u_tok[pi] = t_h
            sgb_tok[pi] = t_h
            t_hs = p.dma("sync", "hsa" if pi == 0 else "hsb",
                         h_dst[:, tc, ot * 512:(ot + 1) * 512],
                         hbl[pi][:], waits=[t_h])
            hbl_tok[pi] = t_hs
            t_h_last = t_hs
        wgq_tok[oi] = mg_last
        wuq_tok[oi] = mu_last
    t_f1_vfence = p.tok("v")
    t_hs_tot = [("hsa", p.counts["hsa"]), ("hsb", p.counts["hsb"])]
    f1.close()

    p.phase_barrier()

    t_f1_scfence = p.tok("sc")
    f3 = ExitStack()
    f3sb = lambda name, shape, dtype=F32: f3.enter_context(
        nc.sbuf_tensor(name, shape, dtype, side="right"))
    h_qT = f3sb("h_qT", [128, NJC, 512], BF16)
    hrow = [f3sb("hrow0", [128, I])]
    hq_tm = [f3sb(f"hq_tm{i}", [128, I], BF16) for i in range(2)]
    rr3 = f3sb("rr3", [128, I // 2])
    hstat = f3sb("hstat", [128, 2])
    hs_row = f3sb("hs_row", [1, 512])
    hsb_sb = f3sb("hsb_sb", [128, 512])
    wdb = [f3sb(f"wdb{i}", [128, 8, 1024], BF16) for i in range(2)]
    yblk = [f3sb(f"yblk{i}", [128, 512]) for i in range(2)]
    pfb = [f3sb(f"pfb{i}", [128, 512]) for i in range(2)]
    ps_y = [f3.enter_context(nc.psum_tensor(f"ps_y{i}", [128, 512], F32))
            for i in range(8)]

    h_src = h_sp.rearrange("(a p) o -> p a o", p=128)
    wdq_src4 = wdq_sp.rearrange("(a p) c -> p a c", p=128)
    hrow_tok = [None]
    hq_tok = [None, None]
    t_hr_prev = []
    t_dn_vlast = t_f1_vfence
    yblk_tok = [None, None]
    pfb_tok = [None, None]
    wdb_tok = [None, None]

    def emit_hquant(tt, gate):
        toks_hs = []
        toks_hx = []
        for sub in range(4):
            tc = tt * 4 + sub
            par = sub % 2
            tl = p.dma("gpsimd", "hrl", hrow[0][:], h_src[:, tc, :],
                       waits=[hrow_tok[0]] +
                             (t_hs_tot if (tt == 0 and sub == 0) else []))
            p.op("vector",
                 lambda eng: eng.reduce_max(
                     hstat[:, 0:1], hrow[0][:], axis=mybir.AxisListType.X,
                     apply_absolute_value=True),
                 waits=[tl, t_f1_scfence], inc="v")
            vpad()
            t_hsv = p.op("vector",
                         lambda eng, sub=sub: eng.tensor_scalar(
                             hs_all[:, sub:sub + 1], hstat[:, 0:1],
                             1.0 / 127.0, 1e-5, ALU.mult, ALU.add),
                         waits=list(t_hr_prev), inc="v")
            toks_hs.append(t_hsv)
            vpad()
            t_rhs = p.op("vector",
                         lambda eng, sub=sub: eng.reciprocal(
                             hstat[:, 1:2], hs_all[:, sub:sub + 1]),
                         inc="v")
            t_hq = None
            for hv in range(2):
                t_rr = p.op("scalar",
                            lambda eng, hv=hv: eng.activation(
                                rr3[:], hrow[0][:, hv * 4096:(hv + 1) * 4096],
                                AF.Copy, bias=C2P23, scale=hstat[:, 1:2]),
                            waits=[t_rhs, t_conj, t_hq], inc="sc")
                t_hq = p.op("vector",
                            lambda eng, hv=hv, par=par: eng.tensor_scalar(
                                hq_tm[par][:, hv * 4096:(hv + 1) * 4096],
                                rr3[:], C2P23, None, ALU.subtract),
                            waits=[t_rr, hq_tok[par]], inc="v")
                hrow_tok[0] = t_rr
            t_hx = p.dma("sync", "hqx",
                         h_qT[:, :, sub * 128:(sub + 1) * 128],
                         hq_tm[par].ap().rearrange("p (a b) -> p a b", b=128),
                         waits=[t_hq] + list(gate), transpose=True)
            hq_tok[par] = t_hx
            toks_hx.append(t_hx)
        return toks_hs, toks_hx

    t_out_toks = []
    hq_state = emit_hquant(0, [])
    for tt in range(4):
        hs_toks, hx_toks = hq_state
        t_hr = []
        for sub in range(4):
            t_hr.append(p.op(
                "tensor",
                lambda eng, sub=sub: eng.matmul(
                    ps_y[0][0:1, sub * 128:(sub + 1) * 128],
                    hs_all[:, sub:sub + 1], ident[:], start=True, stop=True),
                waits=[hs_toks[sub], t_ident, t_dn_vlast, t_f1_scfence],
                inc="te"))
        t_hr_prev[:] = t_hr
        t_hrow = p.op("vector",
                      lambda eng: eng.tensor_scalar(
                          hs_row[0:1, :], ps_y[0][0:1, :], ws_vec[0:1, 2:3],
                          None, ALU.mult),
                      waits=t_hr + [t_wsb], inc="v")
        t_hbc = p.op("tensor",
                     lambda eng: eng.matmul(ps_y[1][:], ones_row[0:1, :],
                                            hs_row[0:1, :], start=True,
                                            stop=True),
                     waits=[t_hrow, t_conj], inc="te")
        t_hsb = p.op("vector",
                     lambda eng: eng.tensor_copy(hsb_sb[:], ps_y[1][:]),
                     waits=[t_hbc], inc="v")
        if os.environ.get("KDBG") and tt == 0:
            dh1 = p.dma("sync", "dbq", dbg_hqT,
                        h_qT.ap().rearrange("p a b -> p (a b)"),
                        waits=[hx_toks[-1]])
            dh2 = p.dma("sync", "dbq", dbg_hsb, hsb_sb[:], waits=[t_hsb])
            for e in ENGINES:
                p.op(e, None, waits=[dh1, dh2])
        for dh in range(2):
            mm_last = [None] * 8
            for jg in range(NJC // 8):
                wi = jg % 2
                twd = p.dma("gpsimd", "wda" if wi == 0 else "wdb2",
                            wdb[wi][:],
                            wdq_src4[:, jg * 8:(jg + 1) * 8,
                                     dh * 1024:(dh + 1) * 1024],
                            waits=[wdb_tok[wi], t_wdq_last])
                for j8 in range(8):
                    jc = jg * 8 + j8
                    for i8 in range(8):
                        mm_last[i8] = p.op(
                            "tensor",
                            lambda eng, jc=jc, j8=j8, i8=i8, wi=wi: eng.matmul(
                                ps_y[i8][:],
                                wdb[wi][:, j8, i8 * 128:(i8 + 1) * 128],
                                h_qT[:, jc, :],
                                start=(jc == 0), stop=(jc == NJC - 1)),
                            waits=([twd, t_hsb, t_dn_vlast,
                                    hx_toks[-1] if jc == 0 else None]
                                   if (i8 == 0 and j8 == 0) else ()),
                            inc="te")
                wdb_tok[wi] = mm_last[7]
            if dh == 1 and tt + 1 < 4:
                hq_state = emit_hquant(tt + 1, [mm_last[7]])
            for i8 in range(8):
                dc = dh * 8 + i8
                bi2 = i8 % 2
                tpf = p.dma("sync", "pfa" if bi2 == 0 else "pfb2",
                            pfb[bi2][:],
                            part_sp[dc * 128:(dc + 1) * 128,
                                    tt * 512:(tt + 1) * 512],
                            waits=[pfb_tok[bi2], t_part[dc][tt]])
                t_y1 = p.op(
                    "vector",
                    lambda eng, i8=i8, bi2=bi2: eng.tensor_tensor(
                        yblk[bi2][:], ps_y[i8][:], hsb_sb[:], ALU.mult),
                    waits=[mm_last[i8], t_hsb, yblk_tok[bi2]], inc="v")
                t_y2 = p.op(
                    "vector",
                    lambda eng, bi2=bi2: eng.tensor_tensor(
                        yblk[bi2][:], pfb[bi2][:], yblk[bi2][:],
                        ALU.subtract),
                    waits=[tpf], inc="v")
                pfb_tok[bi2] = t_y2
                t_o = p.dma("sync", "yoa" if bi2 == 0 else "yob",
                            y_out[dc * 128:(dc + 1) * 128,
                                  tt * 512:(tt + 1) * 512],
                            yblk[bi2][:], waits=[t_y2])
                yblk_tok[bi2] = t_o
                t_out_toks.append(t_o)
            t_dn_vlast = p.tok("v")
    f3.close()

    fin = [(s, p.counts[s]) for s in DMA_SEMS] + [
        ("v", p.counts["v"]), ("sc", p.counts["sc"]), ("te", p.counts["te"])]
    p.op("sync", None, waits=fin)
    p.op("gpsimd", None, waits=[(s, p.counts[s]) for s in DMA_SEMS])

    nums = sorted(h.num for h in p.sem.values())
    lo, hi = nums[0], nums[-1]
    nc.all_engine_barrier()
    nc.gpsimd.dma_reset(range(lo, hi + 1))
    nc.gpsimd.sem_clear(range(lo, hi + 1))
    nc.all_engine_barrier()

    with nc.Block() as block:
        p.emit(block)
    gstack.close()
    return nc


_ncache = {}


def _get_program():
    if "nc" not in _ncache:
        _ncache["nc"] = build_program()
    return _ncache["nc"]


def _rope_tables():
    e = (np.arange(0, HD, 2, dtype=np.float32) / np.float32(HD)).astype(
        np.float32)
    inv = (np.float32(1.0) / np.power(np.float32(10000.0), e)).astype(
        np.float32)
    ang = (np.arange(S, dtype=np.float32)[:, None] * inv[None, :]).astype(
        np.float32)
    return np.cos(ang).astype(np.float32), np.sin(ang).astype(np.float32)


def kernel(L0_attn_x, L0_attn_ln_w, L0_attn_wq, L0_attn_wk, L0_attn_wv,
           L0_attn_wo, L0_ffn_w_gate, L0_ffn_w_up, L0_ffn_w_down):
    x = np.ascontiguousarray(np.asarray(L0_attn_x, dtype=np.float32))
    lnw = np.ascontiguousarray(
        np.asarray(L0_attn_ln_w, dtype=np.float32).reshape(NIC, 128))
    permH = np.concatenate([np.arange(0, 128, 2), np.arange(1, 128, 2)])
    permq = np.concatenate([hh * 128 + permH for hh in range(H)])
    permk = np.concatenate([hh * 128 + permH for hh in range(KH)])
    wq = np.ascontiguousarray(np.asarray(L0_attn_wq, np.float32)[:, permq])
    wk = np.ascontiguousarray(np.asarray(L0_attn_wk, np.float32)[:, permk])
    wv = np.ascontiguousarray(np.asarray(L0_attn_wv, np.float32))
    wo = np.ascontiguousarray(np.asarray(L0_attn_wo, np.float32))
    wgT = np.ascontiguousarray(np.asarray(L0_ffn_w_gate, np.float32).T)
    wuT = np.ascontiguousarray(np.asarray(L0_ffn_w_up, np.float32).T)
    wdT = np.ascontiguousarray(np.asarray(L0_ffn_w_down, np.float32).T)
    pp = np.arange(128)[:, None]
    cc = np.arange(640)[None, :]
    mask = np.where((cc >= pp) & (cc <= pp + W), 0.0, NEG).astype(np.float32)
    ropeC, ropeS = _rope_tables()
    ident = np.eye(128, dtype=np.float32)

    nc = _get_program()
    common = dict(lnw=lnw, wq=wq, wk=wk, wv=wv, wo=wo, wgT=wgT, wuT=wuT,
                  wdT=wdT, mask=mask, ropeC=ropeC, ropeS=ropeS, ident=ident)
    in_maps = []
    for b in range(8):
        m = dict(common)
        m["x"] = np.ascontiguousarray(x[b])
        m["xT"] = np.ascontiguousarray(x[b].T)
        in_maps.append(m)
    res = run_bass_kernel_spmd(nc, in_maps, core_ids=list(range(8)))
    out = np.stack([np.asarray(r["y"], np.float32).T for r in res.results],
                   axis=0)
    return out


# version 08
# speedup vs baseline: 1.2900x; 1.1376x over previous
from contextlib import ExitStack

import numpy as np
import concourse.bass as bass
import concourse.mybir as mybir
from concourse.bass_utils import run_bass_kernel_spmd

F32 = mybir.dt.float32
BF16 = mybir.dt.bfloat16
AF = mybir.ActivationFunctionType
ALU = mybir.AluOpType

S = 2048
D = 2048
H = 16
KH = 4
HD = 128
W = 512
I = 8192
NTC = S // 128
NIC = D // 128
NOC = I // 512
NJC = I // 128
NDC = D // 128
SCL = 1.0 / np.sqrt(128.0)
C2P23 = 12582912.0
NEG = -1.0e30

ENGINES = ("sync", "gpsimd", "vector", "scalar", "tensor")
SEM_ENGINE = {"v": "vector", "sc": "scalar", "te": "tensor", "gp": "gpsimd"}


class Prog:

    def __init__(self, nc):
        self.nc = nc
        self.sem = {}
        self.counts = {}
        self.ops = {e: [] for e in ENGINES}
        self.last_wait = {e: {} for e in ENGINES}
        self.sems_used = set()

    def phase_barrier(self):
        toks = [(s, self.counts[s]) for s in self.sems_used]
        toks += [(e, self.counts[e]) for e in ("v", "sc", "te", "gp")]
        for e in ENGINES:
            self.op(e, None, waits=toks)
        self.sems_used = set()

    def make_sems(self, stack, names):
        for n in names:
            self.sem[n] = stack.enter_context(self.nc.semaphore(n))
            self.counts[n] = 0

    def op(self, eng, fn=None, waits=(), inc=None, incn=1):
        wl = {}
        for t in waits:
            if t is None:
                continue
            s, v = t
            if v <= 0 or SEM_ENGINE.get(s) == eng:
                continue
            if v > wl.get(s, -1):
                wl[s] = v
        wl2 = []
        for s, v in wl.items():
            if v > self.last_wait[eng].get(s, 0):
                self.last_wait[eng][s] = v
                wl2.append((s, v))
        tok = None
        if inc is not None:
            self.counts[inc] += incn
            tok = (inc, self.counts[inc])
        self.ops[eng].append((fn, wl2, inc, incn))
        return tok

    def dma(self, queue, sem, out, in_, waits=(), transpose=False,
            slow=False):
        semname = sem
        self.sems_used.add(sem)

        def fn(eng, out=out, in_=in_, transpose=transpose, slow=slow):
            if transpose:
                return eng.dma_start_transpose(out=out, in_=in_)
            return eng.dma_start(out=out, in_=in_,
                                 allow_slow_non_contiguous=slow)

        return self.op(queue, fn, waits=waits, inc=semname, incn=16)

    def tok(self, sem):
        return (sem, self.counts[sem])

    def emit(self, block):
        for eng_name in ENGINES:
            ops = self.ops[eng_name]
            if not ops:
                continue

            def make_body(ops=ops):
                def body(eng):
                    for fn, wl, inc, incn in ops:
                        for s, v in wl:
                            eng.wait_ge(self.sem[s], v)
                        if fn is not None:
                            ins = fn(eng)
                            if inc is not None:
                                ins.then_inc(self.sem[inc], incn)
                        elif inc is not None:
                            eng.nop().then_inc(self.sem[inc], incn)
                return body

            getattr(block, eng_name)(make_body())


def bcast_mid(ap2d, n):
    dims = [list(q) for q in ap2d.ap]
    return bass.AP(ap2d.tensor, ap2d.offset, [dims[0], [0, n]] + dims[1:])


def build_program():
    nc = bass.Bass("TRN2", target_bir_lowering=False, debug=False)
    dt = nc.dram_tensor
    x_in = dt("x", [S, D], F32, kind="ExternalInput").ap()
    xT_in = dt("xT", [D, S], F32, kind="ExternalInput").ap()
    lnw_in = dt("lnw", [NIC, 128], F32, kind="ExternalInput").ap()
    wq_in = dt("wq", [D, D], F32, kind="ExternalInput").ap()
    wk_in = dt("wk", [D, KH * HD], F32, kind="ExternalInput").ap()
    wv_in = dt("wv", [D, KH * HD], F32, kind="ExternalInput").ap()
    wo_in = dt("wo", [D, D], F32, kind="ExternalInput").ap()
    wgT_in = dt("wgT", [D, I], F32, kind="ExternalInput").ap()
    wuT_in = dt("wuT", [D, I], F32, kind="ExternalInput").ap()
    wdT_in = dt("wdT", [I, D], F32, kind="ExternalInput").ap()
    mask_in = dt("mask", [128, 640], F32, kind="ExternalInput").ap()
    ropeC_in = dt("ropeC", [S, 64], F32, kind="ExternalInput").ap()
    ropeS_in = dt("ropeS", [S, 64], F32, kind="ExternalInput").ap()
    ident_in = dt("ident", [128, 128], F32, kind="ExternalInput").ap()
    import os
    _K = "ExternalOutput" if os.environ.get("KDBG") else "Internal"
    xq_sp = dt("xq_sp", [NIC * 128, S], BF16, kind=_K).ap()
    q_sp = dt("q_sp", [S, D], BF16, kind=_K).ap()
    h_sp = dt("h_sp", [S, I], F32, kind=_K).ap()
    wdq_sp = dt("wdq_sp", [I, D], BF16, kind=_K).ap()
    part_sp = dt("part_sp", [D, S], F32, kind=_K).ap()
    dbg_sp = dt("dbg_sp", [128, 4 * D + 16], F32, kind=_K).ap()
    dbg_ao = dt("dbg_ao", [128, H * S], BF16, kind=_K).ap()
    dbg_kf = dt("dbg_kf", [128, KH * S], BF16, kind=_K).ap()
    dbg_vt = dt("dbg_vt", [128, NTC * KH * HD], BF16, kind=_K).ap()
    dbg_pt = dt("dbg_pt", [128, H * 5 * 128], BF16, kind=_K).ap()
    dbg_ps = dt("dbg_ps", [128, H * 5 * 128], BF16, kind=_K).ap()
    dbg_wob = dt("dbg_wob", [128, NDC * D], BF16, kind=_K).ap()
    dbg_psw = dt("dbg_psw", [128, 512], F32, kind=_K).ap()
    dbg_hqT = dt("dbg_hqT", [128, NJC * 512], BF16, kind=_K).ap()
    dbg_hsb = dt("dbg_hsb", [128, 512], F32, kind=_K).ap()
    y_out = dt("y", [D, S], F32, kind="ExternalOutput").ap()

    gstack = ExitStack()
    p = Prog(nc)
    DMA_SEMS = ["c0", "f0a", "f0b", "xa", "xb", "na", "nb", "qca", "qcb",
                "xsa", "xsb", "wqa", "wqb", "rpa", "rpb", "qspa", "qspb",
                "wkv", "kfa", "kfb", "qta", "qtb", "qfa", "qfb", "ptx",
                "woa", "wob_", "xfa", "xfb_", "pta", "ptb", "xqr", "wgl",
                "wds", "hsa", "hsb", "hrl", "hqx", "wda", "wdb2", "pfa",
                "pfb2", "yoa", "yob", "dbq"]
    p.make_sems(gstack, ["v", "sc", "te", "gp"] + DMA_SEMS)
    sb = lambda name, shape, dtype=F32: gstack.enter_context(
        nc.sbuf_tensor(name, shape, dtype, side="left"))

    ident = sb("ident_s", [128, 128])
    mask = sb("mask_s", [128, 640])
    lnw_sb = sb("lnw_sb", [128, NIC])
    c_2p23 = sb("c_2p23", [128, 1])
    c_2p23p1 = sb("c_2p23p1", [128, 1])
    c_eps5 = sb("c_eps5", [128, 1])
    c_eps6 = sb("c_eps6", [128, 1])
    ones_col = sb("ones_col", [128, 1])
    ones_row = sb("ones_row", [1, 128])
    ws_vec = sb("ws_vec", [128, 4])
    rws_vec = sb("rws_vec", [128, 4])
    s_all = sb("s_all", [128, NTC])
    negA_g = sb("negA_g", [128, NTC])
    negA_u = sb("negA_u", [128, NTC])
    hs_all = sb("hs_all", [128, 4])
    dump = sb("dump", [128, 4096], BF16)
    dump_sc = sb("dump_sc", [128, D], BF16)
    wsacc = sb("wsacc", [128, 52])

    padv = sb("padv", [128, 1])
    pads = sb("pads", [128, 1])

    nc.const_aps.register(F32, C2P23, c_2p23[:, 0:1])

    def vpad():
        p.op("vector", lambda eng: eng.tensor_copy(padv[:], ones_col[:]))

    def spad():
        p.op("scalar", lambda eng: eng.copy(pads[:], ones_col[:]))

    for t, val in ((c_2p23, C2P23), (c_2p23p1, C2P23 + 1.0), (c_eps5, 1e-5),
                   (c_eps6, 1e-6), (ones_col, 1.0), (ones_row, 1.0)):
        p.op("vector", lambda eng, t=t, val=val: eng.memset(t[:], val),
             inc="v")
    t_conj = p.tok("v")
    t_ident = p.dma("sync", "c0", ident[:], ident_in)
    t_mask = p.dma("sync", "c0", mask[:], mask_in)
    t_lnw = p.dma("sync", "c0", lnw_sb[:], lnw_in.rearrange("a p -> p a"),
                  slow=True)
    t_consts = ("c0", p.counts["c0"])
    t_ident = t_mask = t_lnw = t_consts

    a_era = ExitStack()
    xn2_fm = a_era.enter_context(
        nc.sbuf_tensor("xn2_fm", [128, NIC, S], BF16, side="left"))
    a1 = ExitStack()
    a1sb = lambda name, shape, dtype=F32: a1.enter_context(
        nc.sbuf_tensor(name, shape, dtype, side="right"))
    xbuf = [a1sb(f"xbuf{i}", [128, D]) for i in range(2)]
    xcb = a1sb("xcb", [128, D])
    xnb = a1sb("xnb", [128, D])
    rrb = a1sb("rrb", [128, D])
    xn2c = [a1sb(f"xn2c{i}", [128, D], BF16) for i in range(2)]
    xqc = [a1sb(f"xqc{i}", [128, D], BF16) for i in range(2)]
    stat = a1sb("stat", [128, 8])
    xq_col = [a1sb(f"xqcol{i}", [128, NIC, 128], BF16) for i in range(2)]

    xbuf_tok = [None, None]
    xn2c_tok = [None, None]
    xqc_tok = [None, None]
    xqcol_tok = [None, None]
    t_xn2fm = []
    t_xqsp = []
    for tc in range(NTC):
        bi = tc % 2
        xb = xbuf[bi]
        tl = p.dma("sync", "xa" if bi == 0 else "xb", xb[:],
                   x_in[tc * 128:(tc + 1) * 128, :],
                   waits=[xbuf_tok[bi]])
        p.op("vector",
             lambda eng, xb=xb: eng.tensor_scalar(
                 dump[:, 0:D], xb[:], 1.0, 0.0, ALU.mult, ALU.add,
                 accum_out=stat[:, 0:1]),
             waits=[tl], inc="v")
        vpad()
        p.op("vector",
             lambda eng: eng.tensor_scalar(
                 stat[:, 1:2], stat[:, 0:1], 1.0 / D, None, ALU.mult),
             inc="v")
        vpad()
        t_xc = p.op("vector",
                    lambda eng, xb=xb: eng.tensor_scalar(
                        xcb[:], xb[:], stat[:, 1:2], None, ALU.subtract),
                    inc="v")
        xbuf_tok[bi] = t_xc
        t_vs = p.op("scalar",
                    lambda eng: eng.activation(
                        dump_sc[:, 0:D], xcb[:], AF.Square,
                        accum_out=stat[:, 2:3]),
                    waits=[t_xc], inc="sc")
        spad()
        t_sd = p.op("scalar",
                    lambda eng: eng.activation(
                        stat[:, 3:4], stat[:, 2:3], AF.Sqrt,
                        bias=c_eps5[:, 0:1], scale=1.0 / D),
                    waits=[t_conj], inc="sc")
        t_rstd = p.op("vector",
                      lambda eng: eng.reciprocal(stat[:, 4:5], stat[:, 3:4]),
                      waits=[t_sd], inc="v")
        vpad()
        t_xn = p.op("vector",
                    lambda eng: eng.tensor_scalar(
                        xnb[:], xcb[:], stat[:, 4:5], None, ALU.mult),
                    inc="v")
        t_r2s = p.op("scalar",
                     lambda eng: eng.activation(
                         dump_sc[:, 0:D], xnb[:], AF.Square,
                         accum_out=stat[:, 2:3]),
                     waits=[t_xn], inc="sc")
        spad()
        t_sd2 = p.op("scalar",
                     lambda eng: eng.activation(
                         stat[:, 3:4], stat[:, 2:3], AF.Sqrt,
                         bias=c_eps6[:, 0:1], scale=1.0 / D),
                     inc="sc")
        t_r2 = p.op("vector",
                    lambda eng: eng.reciprocal(stat[:, 4:5], stat[:, 3:4]),
                    waits=[t_sd2], inc="v")
        vpad()
        t_xn2 = p.op("vector",
                     lambda eng, o=xn2c[bi]: eng.tensor_scalar(
                         o[:], xnb[:], stat[:, 4:5], None, ALU.mult),
                     waits=[xn2c_tok[bi]], inc="v")
        p.op("vector",
             lambda eng: eng.reduce_max(
                 stat[:, 5:6], xnb[:], axis=mybir.AxisListType.X,
                 apply_absolute_value=True),
             inc="v")
        vpad()
        p.op("vector",
             lambda eng, tc=tc: eng.tensor_scalar(
                 s_all[:, tc:tc + 1], stat[:, 5:6], 1.0 / 127.0, 1e-5,
                 ALU.mult, ALU.add),
             inc="v")
        vpad()
        t_rs = p.op("vector",
                    lambda eng, tc=tc: eng.reciprocal(
                        stat[:, 7:8], s_all[:, tc:tc + 1]),
                    inc="v")
        t_rr = p.op("scalar",
                    lambda eng: eng.activation(
                        rrb[:], xnb[:], AF.Copy, bias=C2P23,
                        scale=stat[:, 7:8]),
                    waits=[t_rs, t_conj], inc="sc")
        t_xq = p.op("vector",
                    lambda eng, o=xqc[bi]: eng.tensor_scalar(
                        o[:], rrb[:], C2P23, None, ALU.subtract),
                    waits=[t_rr, xqc_tok[bi]], inc="v")
        if tc == 0 and os.environ.get("KDBG"):
            dts = [p.dma("sync", "dbq", dbg_sp[:, 0:D], xbuf[bi][:], waits=[t_xq]),
                   p.dma("sync", "dbq", dbg_sp[:, D:2 * D], xcb[:], waits=[t_xq]),
                   p.dma("sync", "dbq", dbg_sp[:, 2 * D:3 * D], xnb[:],
                         waits=[t_xq]),
                   p.dma("sync", "dbq", dbg_sp[:, 3 * D:4 * D], rrb[:],
                         waits=[t_xq]),
                   p.dma("sync", "dbq", dbg_sp[:, 4 * D:4 * D + 8], stat[:, 0:8],
                         waits=[t_xq])]
            p.op("vector", None, waits=dts)
            p.op("scalar", None, waits=dts)
        t_x1 = p.dma("sync", "na" if bi == 0 else "nb",
                     xn2_fm[:, :, tc * 128:(tc + 1) * 128],
                     xn2c[bi].ap().rearrange("p (a b) -> p a b", b=128),
                     waits=[t_xn2], transpose=True)
        xn2c_tok[bi] = t_x1
        t_x2 = p.dma("sync", "qca" if bi == 0 else "qcb", xq_col[bi][:],
                     xqc[bi].ap().rearrange("p (a b) -> p a b", b=128),
                     waits=[t_xq, xqcol_tok[bi]], transpose=True)
        xqc_tok[bi] = t_x2
        t_sp = p.dma("gpsimd", "xsa" if bi == 0 else "xsb",
                     xq_sp.rearrange("(a p) t -> p a t", p=128)[
                         :, :, tc * 128:(tc + 1) * 128],
                     xq_col[bi][:], waits=[t_x2])
        xqcol_tok[bi] = t_sp
        t_xn2fm.append(t_x1)
        t_xqsp.append(t_sp)
    t_xqsp_tot = [("xsa", p.counts["xsa"]), ("xsb", p.counts["xsb"])]
    a1.close()
    p.phase_barrier()

    f0 = ExitStack()
    wbuf = [f0.enter_context(nc.sbuf_tensor(f"f0w{i}", [128, I], F32,
                                                side="right"))
            for i in range(2)]
    f0dump = f0.enter_context(nc.sbuf_tensor("f0dump", [128, I], BF16,
                                             side="right"))
    ps_ws = f0.enter_context(nc.psum_tensor("ps_ws", [128, 4], F32))
    ws_row = f0.enter_context(nc.sbuf_tensor("ws_row", [1, 4], F32,
                                                 side="right"))
    rws_row = f0.enter_context(nc.sbuf_tensor("rws_row", [1, 4], F32,
                                              side="right"))
    srcs = [
        (wgT_in.rearrange("(a p) c -> p a c", p=128), NIC, 0),
        (wuT_in.rearrange("(a p) c -> p a c", p=128), NIC, 16),
        (wdT_in.rearrange("(c p) d -> p c d", p=128), 16, 32),
    ]
    buf_tok = [None, None]
    t_sum_toks = []
    idx = 0
    for mi, (src, nch, base) in enumerate(srcs):
        for c in range(nch):
            bi = idx % 2
            if mi < 2:
                chunk = src[:, c, :]
                dst = wbuf[bi][:]
            else:
                chunk = src[:, c * 4:(c + 1) * 4, :]
                dst = wbuf[bi].ap().rearrange("p (a b) -> p a b", b=2048)
            tl = p.dma("gpsimd", "f0a" if bi == 0 else "f0b", dst, chunk,
                       waits=[buf_tok[bi]])
            buf_tok[bi] = p.op(
                "scalar",
                lambda eng, b=wbuf[bi], k=base + c: eng.activation(
                    f0dump[:], b[:], AF.Abs,
                    accum_out=wsacc[:, k:k + 1]),
                waits=[tl], inc="sc")
            idx += 1
    t_red = []
    for mi, (lo, hi) in enumerate(((0, 16), (16, 32), (32, 48))):
        t_red.append(p.op(
            "vector",
            lambda eng, mi=mi, lo=lo, hi=hi: eng.reduce_sum(
                wsacc[:, 48 + mi:49 + mi], wsacc[:, lo:hi],
                axis=mybir.AxisListType.X),
            waits=[buf_tok[0], buf_tok[1]], inc="v"))
    t_mm_ws = []
    for mi in range(3):
        t_mm_ws.append(p.op(
            "tensor",
            lambda eng, mi=mi: eng.matmul(
                ps_ws[0:1, mi:mi + 1], ones_col[:, 0:1],
                wsacc[:, 48 + mi:49 + mi], start=True, stop=True),
            waits=[t_red[mi], t_conj], inc="te"))
    t_wsr = p.op("vector",
                 lambda eng: eng.tensor_scalar(
                     ws_row[0:1, 0:3], ps_ws[0:1, 0:3], 1.0 / 16777216.0,
                     1e-5, ALU.mult, ALU.add),
                 waits=t_mm_ws, inc="v")
    vpad()
    t_rwsr = p.op("vector",
                  lambda eng: eng.reciprocal(rws_row[0:1, 0:3],
                                             ws_row[0:1, 0:3]),
                  waits=[t_wsr], inc="v")
    t_bc1 = p.op("tensor",
                 lambda eng: eng.matmul(ps_ws[:, 0:3], ones_row[0:1, :],
                                        ws_row[0:1, 0:3], start=True,
                                        stop=True),
                 waits=[t_rwsr, t_conj], inc="te")
    t_wsb = p.op("vector",
                 lambda eng: eng.tensor_copy(ws_vec[:, 0:3], ps_ws[:, 0:3]),
                 waits=[t_bc1], inc="v")
    t_bc2 = p.op("tensor",
                 lambda eng: eng.matmul(ps_ws[:, 0:3], ones_row[0:1, :],
                                        rws_row[0:1, 0:3], start=True,
                                        stop=True),
                 waits=[t_wsb], inc="te")
    t_rwsb = p.op("vector",
                  lambda eng: eng.tensor_copy(rws_vec[:, 0:3],
                                              ps_ws[:, 0:3]),
                  waits=[t_bc2], inc="v")

    t_nA = []
    t_nA.append(p.op("vector",
                     lambda eng: eng.tensor_scalar(
                         negA_g[:], s_all[:], ws_vec[:, 0:1], -1.0, ALU.mult,
                         ALU.mult),
                     waits=[t_wsb], inc="v"))
    t_nA.append(p.op("vector",
                     lambda eng: eng.tensor_scalar(
                         negA_u[:], s_all[:], ws_vec[:, 1:2], -1.0, ALU.mult,
                         ALU.mult),
                     inc="v"))
    f0.close()

    p.phase_barrier()

    a2 = ExitStack()
    a2sb = lambda name, shape, dtype=F32: a2.enter_context(
        nc.sbuf_tensor(name, shape, dtype, side="right"))
    wqf = [a2sb(f"wqf{i}", [128, NIC, 512]) for i in range(2)]
    wqb = a2sb("wqb", [128, NIC, 512], BF16)
    ropeC = [a2sb(f"ropeC{i}", [128, 64]) for i in range(2)]
    ropeS = [a2sb(f"ropeS{i}", [128, 64]) for i in range(2)]
    rope_sc = [a2sb(f"ropesc{i}", [128, 256]) for i in range(4)]
    qrot = [a2sb(f"qrot{i}", [128, 512], BF16) for i in range(2)]
    ps_q = [a2.enter_context(nc.psum_tensor(f"ps_q{i}", [128, 512], F32))
            for i in range(2)]

    def emit_rope(ps, nheads, Cb, Sb, scr, out_bf, waits):
        ve = ps.ap().rearrange("p (h c) -> p h c", c=128)[:, :, 0:64]
        vo = ps.ap().rearrange("p (h c) -> p h c", c=128)[:, :, 64:128]
        Cb3 = bcast_mid(Cb, nheads)
        Sb3 = bcast_mid(Sb, nheads)
        vws = [scr[i].ap().rearrange("p (h c) -> p h c", c=64)[:, 0:nheads]
               for i in range(4)]
        oe = out_bf.ap().rearrange("p (h c) -> p h c", c=128)[:, :, 0:64]
        oo = out_bf.ap().rearrange("p (h c) -> p h c", c=128)[:, :, 64:128]
        p.op("vector", lambda eng: eng.tensor_tensor(vws[0], ve, Cb3,
                                                     ALU.mult),
             waits=waits, inc="v")
        p.op("vector", lambda eng: eng.tensor_tensor(vws[1], vo, Sb3,
                                                     ALU.mult), inc="v")
        p.op("vector", lambda eng: eng.tensor_tensor(vws[2], ve, Sb3,
                                                     ALU.mult), inc="v")
        p.op("vector", lambda eng: eng.tensor_tensor(vws[3], vo, Cb3,
                                                     ALU.mult), inc="v")
        p.op("vector", lambda eng: eng.tensor_tensor(oe, vws[0], vws[1],
                                                     ALU.subtract), inc="v")
        return p.op("vector", lambda eng: eng.tensor_tensor(oo, vws[2],
                                                            vws[3], ALU.add),
                    inc="v")

    wq_src = wq_in.rearrange("(a p) o -> p a o", p=128)
    ropeb_tok = [None, None]
    qrot_tok = [None, None]
    psq_tok = [None, None]
    wqf_tok = [None, None]
    t_last_qmm = None
    t_qsp = [None] * NTC
    for qo in range(4):
        wi = qo % 2
        twl = p.dma("gpsimd", "wqa" if wi == 0 else "wqb", wqf[wi][:],
                    wq_src[:, :, qo * 512:(qo + 1) * 512],
                    waits=[wqf_tok[wi]])
        casts = []
        for ic in range(NIC):
            casts.append(p.op(
                "vector",
                lambda eng, ic=ic, wi=wi: eng.tensor_scalar(
                    wqb[:, ic, :], wqf[wi][:, ic, :], lnw_sb[:, ic:ic + 1],
                    None, ALU.mult),
                waits=[twl, t_lnw, t_last_qmm, t_nA[-1]], inc="v"))
        wqf_tok[wi] = casts[-1]
        t_cast = casts[-1]
        for tc in range(NTC):
            ri = tc % 2
            pi = tc % 2
            rsem = "rpa" if ri == 0 else "rpb"
            tC = p.dma("sync", rsem, ropeC[ri][:],
                       ropeC_in[tc * 128:(tc + 1) * 128, :],
                       waits=[ropeb_tok[ri]])
            tS = p.dma("sync", rsem, ropeS[ri][:],
                       ropeS_in[tc * 128:(tc + 1) * 128, :])
            mm_last = None
            for ic in range(NIC):
                mm_last = p.op(
                    "tensor",
                    lambda eng, ic=ic, tc=tc, pi=pi: eng.matmul(
                        ps_q[pi][:], xn2_fm[:, ic, tc * 128:(tc + 1) * 128],
                        wqb[:, ic, :], start=(ic == 0), stop=(ic == NIC - 1)),
                    waits=([t_cast, psq_tok[pi], t_xn2fm[tc]]
                           if ic == 0 else ()),
                    inc="te")
            t_last_qmm = mm_last
            t_rope = emit_rope(ps_q[pi], 4, ropeC[ri][:], ropeS[ri][:],
                               rope_sc, qrot[pi],
                               [mm_last, tC, tS, qrot_tok[pi]])
            ropeb_tok[ri] = t_rope
            psq_tok[pi] = t_rope
            tqs = p.dma("gpsimd", "qspa" if pi == 0 else "qspb",
                        q_sp[tc * 128:(tc + 1) * 128,
                             qo * 512:(qo + 1) * 512],
                        qrot[pi][:], waits=[t_rope])
            qrot_tok[pi] = tqs
            t_qsp[tc] = tqs
    a2.close()

    p.phase_barrier()

    a3 = ExitStack()
    a3sb = lambda name, shape, dtype=F32: a3.enter_context(
        nc.sbuf_tensor(name, shape, dtype, side="right"))
    kv_era = ExitStack()
    k_fm = kv_era.enter_context(
        nc.sbuf_tensor("k_fm", [128, KH, S], BF16, side="right"))
    v_tm = kv_era.enter_context(
        nc.sbuf_tensor("v_tm", [128, NTC, KH * HD], BF16, side="right"))
    wkf = a3sb("wkf", [128, NIC, 512])
    wkb = a3sb("wkb", [128, NIC, 512], BF16)
    wvb = a3sb("wvb", [128, NIC, 512], BF16)
    ropeCk = [a3sb(f"ropeCk{i}", [128, 64]) for i in range(2)]
    ropeSk = [a3sb(f"ropeSk{i}", [128, 64]) for i in range(2)]
    rope_sck = [a3sb(f"ropesck{i}", [128, 256]) for i in range(4)]
    krot = [a3sb(f"krot{i}", [128, 512], BF16) for i in range(2)]
    ps_k = [a3.enter_context(nc.psum_tensor(f"ps_k{i}", [128, 512], F32))
            for i in range(2)]
    ps_v = [a3.enter_context(nc.psum_tensor(f"ps_v{i}", [128, 512], F32))
            for i in range(2)]

    twk = p.dma("gpsimd", "wkv", wkf[:], wk_in.rearrange("(a p) o -> p a o", p=128))
    ckl = None
    for ic in range(NIC):
        ckl = p.op(
            "vector",
            lambda eng, ic=ic: eng.tensor_scalar(
                wkb[:, ic, :], wkf[:, ic, :], lnw_sb[:, ic:ic + 1], None,
                ALU.mult),
            waits=[twk], inc="v")
    twv = p.dma("gpsimd", "wkv", wkf[:], wv_in.rearrange("(a p) o -> p a o", p=128),
                waits=[ckl])
    for ic in range(NIC):
        ckl = p.op(
            "vector",
            lambda eng, ic=ic: eng.tensor_scalar(
                wvb[:, ic, :], wkf[:, ic, :], lnw_sb[:, ic:ic + 1], None,
                ALU.mult),
            waits=[twv], inc="v")
    t_wkvb = ckl

    ropek_tok = [None, None]
    krot_tok = [None, None]
    psk_tok = [None, None]
    psv_tok = [None, None]
    kfm_tok = [None, None]
    t_kx_l = [None] * NTC
    t_vev_l = [None] * NTC
    for tc in range(NTC):
        ri = tc % 2
        pi = tc % 2
        rsem = "rpa" if ri == 0 else "rpb"
        tC = p.dma("sync", rsem, ropeCk[ri][:],
                   ropeC_in[tc * 128:(tc + 1) * 128, :],
                   waits=[ropek_tok[ri]])
        tS = p.dma("sync", rsem, ropeSk[ri][:],
                   ropeS_in[tc * 128:(tc + 1) * 128, :])
        mmk_last = None
        mmv_last = None
        for ic in range(NIC):
            mmk_last = p.op(
                "tensor",
                lambda eng, ic=ic, tc=tc, pi=pi: eng.matmul(
                    ps_k[pi][:], xn2_fm[:, ic, tc * 128:(tc + 1) * 128],
                    wkb[:, ic, :], start=(ic == 0), stop=(ic == NIC - 1)),
                waits=([t_wkvb, psk_tok[pi]] if ic == 0 else ()),
                inc="te")
            mmv_last = p.op(
                "tensor",
                lambda eng, ic=ic, tc=tc, pi=pi: eng.matmul(
                    ps_v[pi][:], xn2_fm[:, ic, tc * 128:(tc + 1) * 128],
                    wvb[:, ic, :], start=(ic == 0), stop=(ic == NIC - 1)),
                waits=([psv_tok[pi]] if ic == 0 else ()),
                inc="te")
        t_ropek = emit_rope(ps_k[pi], KH, ropeCk[ri][:], ropeSk[ri][:],
                            rope_sck, krot[pi],
                            [mmk_last, tC, tS, krot_tok[pi]])
        ropek_tok[ri] = t_ropek
        psk_tok[pi] = t_ropek
        t_vev = p.op("vector",
                     lambda eng, tc=tc, pi=pi: eng.tensor_copy(
                         v_tm[:, tc, :], ps_v[pi][:]),
                     waits=[mmv_last], inc="v")
        psv_tok[pi] = t_vev
        t_kx = p.dma("sync", "kfa" if pi == 0 else "kfb",
                     k_fm[:, :, tc * 128:(tc + 1) * 128],
                     krot[pi].ap().rearrange("p (a b) -> p a b", b=128),
                     waits=[t_ropek, kfm_tok[pi]], transpose=True)
        kfm_tok[pi] = t_kx
        krot_tok[pi] = t_kx
        t_kx_l[tc] = t_kx
        t_vev_l[tc] = t_vev
    t_a3_vfence = p.tok("v")
    a3.close()
    a_era.close()

    p.phase_barrier()

    b_era = ExitStack()
    attn_out = b_era.enter_context(
        nc.sbuf_tensor("attn_out", [128, H, S], BF16, side="left"))
    bph = ExitStack()
    bsb = lambda name, shape, dtype=F32: bph.enter_context(
        nc.sbuf_tensor(name, shape, dtype, side="right"))
    qc_tm = [bsb(f"qc_tm{i}", [128, D], BF16) for i in range(2)]
    qc_fm = [bsb(f"qc_fm{i}", [128, H, 128], BF16) for i in range(2)]
    sc_f32 = [bsb(f"sc_f32{i}", [128, 640]) for i in range(2)]
    neg_sc = [bsb(f"neg_sc{i}", [128, 640]) for i in range(2)]
    expb = [bsb(f"expb{i}", [128, 640]) for i in range(2)]
    smstat = bsb("smstat", [128, 4])
    probs = [bsb("probs0", [128, H, 5, 128], BF16)] * 2
    probsT = [bsb("probsT0", [128, H, 5, 128], BF16)] * 2
    ps_s = [bph.enter_context(nc.psum_tensor(f"ps_s{i}", [128, 1024], F32))
            for i in range(2)]
    ps_av = bph.enter_context(nc.psum_tensor("ps_av", [128, 2048], F32))

    pm = p.op("vector", lambda eng: eng.memset(
        probs[0].ap().rearrange("p a b c -> p (a b c)"), 0.0), inc="v")
    qctm_tok = [None, None]
    qcfm_tok = [None, None]
    probs_tok = [pm, pm]
    probsT_tok = [None, None]
    ps_s_tok = [None, None]
    ps_av_tok = None
    t_b_vlast = None
    for tc in range(NTC):
        bi = tc % 2
        nkc = min(tc + 1, 5)
        kc0 = tc - nkc + 1
        wcols = nkc * 128
        c0 = kc0 * 128
        moff = 640 - wcols
        tq = p.dma("sync", "qta" if bi == 0 else "qtb", qc_tm[bi][:],
                   q_sp[tc * 128:(tc + 1) * 128, :],
                   waits=[qctm_tok[bi], t_qsp[tc]])
        tqx = p.dma("sync", "qfa" if bi == 0 else "qfb", qc_fm[bi][:],
                    qc_tm[bi].ap().rearrange("p (a b) -> p a b", b=128),
                    waits=[tq, qcfm_tok[bi]], transpose=True)
        qctm_tok[bi] = tqx
        ev_toks = []

        def finalize(h, t_ex):
            hp = h % 2
            t_rd = p.op("vector",
                        lambda eng, hp=hp: eng.reciprocal(
                            smstat[:, 2 + hp:3 + hp],
                            smstat[:, 2 + hp:3 + hp]),
                        waits=[t_ex], inc="v")
            vpad()
            t_pr = p.op(
                "vector",
                lambda eng, h=h, hp=hp, bi=bi, nkc=nkc, wcols=wcols:
                eng.tensor_scalar(
                    probs[bi][:, h, 5 - nkc:5, :].rearrange(
                        "p a b -> p (a b)"),
                    expb[hp][:, 0:wcols], smstat[:, 2 + hp:3 + hp], None,
                    ALU.mult),
                waits=[probs_tok[bi]], inc="v")
            ev_toks.append(t_pr)

        prev = None
        for h in range(H):
            si = h % 2
            hp = h % 2
            kh = h // 4
            t_mm = []
            n_done = 0
            while n_done < wcols:
                n = min(512, wcols - n_done)
                first = (n_done == 0)
                t_mm.append(p.op(
                    "tensor",
                    lambda eng, h=h, kh=kh, si=si, n=n, nd=n_done, c0=c0,
                    bi=bi: eng.matmul(
                        ps_s[si][:, nd:nd + n], qc_fm[bi][:, h, :],
                        k_fm[:, kh, c0 + nd:c0 + nd + n],
                        start=True, stop=True),
                    waits=([tqx, ps_s_tok[si], t_kx_l[tc], t_vev_l[tc],
                            t_a3_vfence] if first else ()),
                    inc="te"))
                n_done += n
            t_ev = p.op(
                "vector",
                lambda eng, wcols=wcols, moff=moff, si=si, hp=hp:
                eng.tensor_tensor(
                    sc_f32[hp][:, 0:wcols], ps_s[si][:, 0:wcols],
                    mask[:, moff:moff + wcols], ALU.add),
                waits=t_mm + [t_mask], inc="v")
            ps_s_tok[si] = t_ev
            t_ng = p.op(
                "vector",
                lambda eng, wcols=wcols, hp=hp: eng.tensor_scalar(
                    neg_sc[hp][:, 0:wcols], sc_f32[hp][:, 0:wcols], -SCL,
                    3.0e38, ALU.mult, ALU.min,
                    accum_out=smstat[:, hp:hp + 1]),
                inc="v")
            t_ex = p.op(
                "scalar",
                lambda eng, wcols=wcols, hp=hp: eng.activation(
                    expb[hp][:, 0:wcols], neg_sc[hp][:, 0:wcols], AF.Exp,
                    bias=smstat[:, hp:hp + 1], scale=-1.0,
                    accum_out=smstat[:, 2 + hp:3 + hp]),
                waits=[t_ng], inc="sc")
            if prev is not None:
                finalize(*prev)
            prev = (h, t_ex)
        finalize(*prev)
        probs_tok[0] = probs_tok[1] = ev_toks[-1]
        qcfm_tok[bi] = t_mm[-1]
        t_px = p.dma("sync", "ptx", probsT[bi][:], probs[bi][:],
                     waits=ev_toks + [probsT_tok[bi]], transpose=True)
        probs_tok[0] = probs_tok[1] = t_px
        if os.environ.get("KDBG") and tc == int(os.environ.get("DBGTC", 15)):
            d1 = p.dma("sync", "dbq", dbg_ps,
                       probs[bi].ap().rearrange("p a b c -> p (a b c)"),
                       waits=[t_px])
            d2 = p.dma("sync", "dbq", dbg_pt,
                       probsT[bi].ap().rearrange("p a b c -> p (a b c)"),
                       waits=[t_px])
            p.op("vector", None, waits=[d1, d2])
            p.op("tensor", None, waits=[d1, d2])
        t_avmm = None
        for h in range(H):
            kh = h // 4
            for i_kc in range(nkc):
                kci = 5 - nkc + i_kc
                kc_abs = kc0 + i_kc
                first = (h == 0 and i_kc == 0)
                t_avmm = p.op(
                    "tensor",
                    lambda eng, h=h, kh=kh, kci=kci, kc_abs=kc_abs, bi=bi,
                    i_kc=i_kc, nkc=nkc: eng.matmul(
                        ps_av[:, h * 128:(h + 1) * 128],
                        v_tm[:, kc_abs, kh * 128:(kh + 1) * 128],
                        probsT[bi][:, h, kci, :],
                        start=(i_kc == 0), stop=(i_kc == nkc - 1)),
                    waits=([t_px, ps_av_tok] if first else ()),
                    inc="te")
        probsT_tok[0] = probsT_tok[1] = t_avmm
        evs = []
        for g in range(4):
            evs.append(p.op(
                "vector",
                lambda eng, g=g, tc=tc: eng.tensor_copy(
                    attn_out[:, 4 * g:4 * (g + 1), tc * 128:(tc + 1) * 128],
                    ps_av[:, g * 512:(g + 1) * 512].rearrange(
                        "p (a b) -> p a b", b=128)),
                waits=[t_avmm], inc="v"))
        ps_av_tok = evs[-1]
        t_b_vlast = evs[-1]
    if os.environ.get("KDBG"):
        p.dma("sync", "dbq", dbg_ao, attn_out.ap().rearrange("p a b -> p (a b)"),
              waits=[t_b_vlast])
        p.dma("sync", "dbq", dbg_kf, k_fm.ap().rearrange("p a b -> p (a b)"),
              waits=[t_b_vlast])
        p.dma("sync", "dbq", dbg_vt, v_tm.ap().rearrange("p a b -> p (a b)"),
              waits=[t_b_vlast])
    bph.close()
    kv_era.close()

    p.phase_barrier()

    wop = ExitStack()
    wsb_ = lambda name, shape, dtype=F32: wop.enter_context(
        nc.sbuf_tensor(name, shape, dtype, side="right"))
    wof = [wsb_(f"wof{i}", [128, D]) for i in range(2)]
    wob = wsb_("wob", [128, NDC, D], BF16)
    xfb = [wsb_(f"xfb{i}", [128, 512]) for i in range(2)]
    pblk = [wsb_(f"pblk{i}", [128, 512]) for i in range(2)]
    ps_wo = [wop.enter_context(nc.psum_tensor(f"ps_wo{i}", [128, 512], F32))
             for i in range(2)]
    wof_tok = [None, None]
    t_wob = None
    for ic in range(NIC):
        wi = ic % 2
        twl = p.dma("gpsimd", "woa" if wi == 0 else "wob_", wof[wi][:],
                    wo_in[ic * 128:(ic + 1) * 128, :], waits=[wof_tok[wi]])
        t_wob = p.op("vector",
                     lambda eng, ic=ic, wi=wi: eng.tensor_copy(
                         wob[:, ic, :], wof[wi][:]),
                     waits=[twl], inc="v")
        wof_tok[wi] = t_wob
    xfb_tok = [None, None]
    pblk_tok = [None, None]
    ps_wo_tok = [None, None]
    t_part = [[None] * 4 for _ in range(NDC)]
    for dc in range(NDC):
        for tt in range(4):
            pi = (dc * 4 + tt) % 2
            mm_last = None
            for h in range(H):
                mm_last = p.op(
                    "tensor",
                    lambda eng, h=h, dc=dc, tt=tt, pi=pi: eng.matmul(
                        ps_wo[pi][:], wob[:, h, dc * 128:(dc + 1) * 128],
                        attn_out[:, h, tt * 512:(tt + 1) * 512],
                        start=(h == 0), stop=(h == H - 1)),
                    waits=([t_wob, ps_wo_tok[pi], t_b_vlast]
                           if h == 0 else ()),
                    inc="te")
            if os.environ.get("KDBG2") and dc == 0 and tt == 0:
                dw = p.dma("sync", "dbq", dbg_wob,
                           wob.ap().rearrange("p a b -> p (a b)"),
                           waits=[t_wob])
                for e in ("tensor", "vector", "scalar", "gpsimd"):
                    p.op(e, None, waits=[dw])
            tx = p.dma("sync", "xfa" if pi == 0 else "xfb_", xfb[pi][:],
                       xT_in[dc * 128:(dc + 1) * 128,
                             tt * 512:(tt + 1) * 512],
                       waits=[xfb_tok[pi]])
            t_add = p.op(
                "vector",
                lambda eng, pi=pi: eng.tensor_tensor(
                    pblk[pi][:], ps_wo[pi][:], xfb[pi][:], ALU.add),
                waits=[mm_last, tx, pblk_tok[pi]], inc="v")
            ps_wo_tok[pi] = t_add
            xfb_tok[pi] = t_add
            t_st = p.dma("sync", "pta" if pi == 0 else "ptb",
                         part_sp[dc * 128:(dc + 1) * 128,
                                 tt * 512:(tt + 1) * 512],
                         pblk[pi][:], waits=[t_add])
            pblk_tok[pi] = t_st
            t_part[dc][tt] = t_st
    t_wo_vfence = p.tok("v")
    wop.close()
    b_era.close()

    p.phase_barrier()

    f1 = ExitStack()
    fsb = lambda name, shape, dtype=F32: f1.enter_context(
        nc.sbuf_tensor(name, shape, dtype, side="right"))
    xq_rd = fsb("xq_rd", [128, NIC, S], BF16)
    wgf = fsb("wgf", [128, NIC, 512])
    wgq = [fsb(f"wgq{i}", [128, NIC, 512], BF16) for i in range(2)]
    wuq = [fsb(f"wuq{i}", [128, NIC, 512], BF16) for i in range(2)]
    wdq = fsb("wdq", [128, 1, D], BF16)
    tq1 = fsb("tq1", [128, NIC * 256])
    sgb = [fsb(f"sgb{i}", [128, 512]) for i in range(2)]
    hbl = [fsb(f"hbl{i}", [128, 512]) for i in range(2)]
    ps_g = [f1.enter_context(nc.psum_tensor(f"ps_g{i}", [128, 512], F32))
            for i in range(2)]
    ps_u = [f1.enter_context(nc.psum_tensor(f"ps_u{i}", [128, 512], F32))
            for i in range(2)]

    t_xqrd = p.dma("gpsimd", "xqr", xq_rd[:],
                   xq_sp.rearrange("(a p) t -> p a t", p=128),
                   waits=t_xqsp_tot)

    tq1_tok = [None]

    def emit_quant(src_f32, dst_bf, rws_slice, waits, nfree):
        t1 = p.op("scalar",
                  lambda eng, src=src_f32, nf=nfree, r=rws_slice:
                  eng.activation(tq1[:, 0:nf], src, AF.Copy, bias=C2P23,
                                 scale=r),
                  waits=list(waits) + [t_conj, t_rwsb, tq1_tok[0]],
                  inc="sc")
        t2 = p.op("scalar",
                  lambda eng, nf=nfree: eng.activation(
                      tq1[:, 0:nf], tq1[:, 0:nf], AF.Relu,
                      bias=c_2p23p1[:, 0:1], scale=-1.0),
                  inc="sc")
        t3 = p.op("vector",
                  lambda eng, dst=dst_bf, nf=nfree: eng.tensor_scalar(
                      dst, tq1[:, 0:nf], 2.0, 1.0, ALU.min, ALU.subtract),
                  waits=[t2], inc="v")
        tq1_tok[0] = t3
        return t3

    h_dst = h_sp.rearrange("(a p) o -> p a o", p=128)
    wgT_src = wgT_in.rearrange("(a p) c -> p a c", p=128)
    wuT_src = wuT_in.rearrange("(a p) c -> p a c", p=128)
    wdT_src = wdT_in.rearrange("(a p) c -> p a c", p=128)
    wdq_dst = wdq_sp.rearrange("(a p) c -> p a c", p=128)
    wgq_tok = [None, None]
    wuq_tok = [None, None]
    wdq_tok = [None]
    sgb_tok = [None, None]
    hbl_tok = [None, None]
    ps_g_tok = [None, None]
    ps_u_tok = [None, None]
    t_h_last = None
    t_wdq_last = None
    def make_wq_stages(ot, plan):
        oi = ot % 2
        st = []

        def s_loadg():
            plan["tg"] = p.dma("gpsimd", "wgl", wgf[:],
                               wgT_src[:, :, ot * 512:(ot + 1) * 512],
                               waits=[tq1_tok[0]])
        st.append(s_loadg)
        for hv2 in range(2):
            def s_qg(hv2=hv2):
                plan["tgq"] = emit_quant(
                    wgf.ap()[:, hv2 * 8:(hv2 + 1) * 8, :].rearrange(
                        "p a c -> p (a c)"),
                    wgq[oi].ap()[:, hv2 * 8:(hv2 + 1) * 8, :].rearrange(
                        "p a c -> p (a c)"),
                    rws_vec[:, 0:1], [plan["tg"], wgq_tok[oi]], NIC * 256)
            st.append(s_qg)

        def s_loadu():
            plan["tu"] = p.dma("gpsimd", "wgl", wgf[:],
                               wuT_src[:, :, ot * 512:(ot + 1) * 512],
                               waits=[plan["tgq"]])
        st.append(s_loadu)
        for hv2 in range(2):
            def s_qu(hv2=hv2):
                plan["tuq"] = emit_quant(
                    wgf.ap()[:, hv2 * 8:(hv2 + 1) * 8, :].rearrange(
                        "p a c -> p (a c)"),
                    wuq[oi].ap()[:, hv2 * 8:(hv2 + 1) * 8, :].rearrange(
                        "p a c -> p (a c)"),
                    rws_vec[:, 1:2], [plan["tu"], wuq_tok[oi]], NIC * 256)
            st.append(s_qu)
        wgf4 = wgf.ap().rearrange("p a c -> p (a c)").rearrange(
            "p (a c) -> p a c", c=D)

        def s_loadd():
            plan["td"] = p.dma("gpsimd", "wgl", wgf4,
                               wdT_src[:, ot * 4:(ot + 1) * 4, :],
                               waits=[tq1_tok[0], plan["tuq"]])
        st.append(s_loadd)
        for hv in range(4):
            def s_qd(hv=hv):
                tdq = emit_quant(
                    wgf4[:, hv:hv + 1, :].rearrange("p a c -> p (a c)"),
                    wdq.ap().rearrange("p a c -> p (a c)"),
                    rws_vec[:, 2:3], [plan["td"], wdq_tok[0]], D)
                tds = p.dma("sync", "wds",
                            wdq_dst[:, ot * 4 + hv:ot * 4 + hv + 1, :],
                            wdq[:], waits=[tdq])
                wdq_tok[0] = tds
                tok_wdq[0] = tds
            st.append(s_qd)
        return st

    tok_wdq = [None]
    plans = {0: {}}
    for s in make_wq_stages(0, plans[0]):
        s()
    for ot in range(NOC):
        oi = ot % 2
        stages = []
        if ot + 1 < NOC:
            plans[ot + 1] = {}
            stages = make_wq_stages(ot + 1, plans[ot + 1])
        tgq = plans[ot]["tgq"]
        tuq = plans[ot]["tuq"]
        t_wdq_last = tok_wdq[0]
        mg_last = None
        mu_last = None
        for tc in range(NTC):
            pi = tc % 2
            for ic in range(NIC):
                mg_last = p.op(
                    "tensor",
                    lambda eng, ic=ic, tc=tc, pi=pi, oi=oi: eng.matmul(
                        ps_g[pi][:], xq_rd[:, ic, tc * 128:(tc + 1) * 128],
                        wgq[oi][:, ic, :], start=(ic == 0),
                        stop=(ic == NIC - 1)),
                    waits=([tgq, t_xqrd, ps_g_tok[pi]] if ic == 0 else ()),
                    inc="te")
                mu_last = p.op(
                    "tensor",
                    lambda eng, ic=ic, tc=tc, pi=pi, oi=oi: eng.matmul(
                        ps_u[pi][:], xq_rd[:, ic, tc * 128:(tc + 1) * 128],
                        wuq[oi][:, ic, :], start=(ic == 0),
                        stop=(ic == NIC - 1)),
                    waits=([tuq, ps_u_tok[pi]] if ic == 0 else ()),
                    inc="te")
            t_sg = p.op(
                "scalar",
                lambda eng, tc=tc, pi=pi: eng.activation(
                    sgb[pi][:], ps_g[pi][:], AF.Silu,
                    scale=negA_g[:, tc:tc + 1]),
                waits=[mg_last, t_nA[0], sgb_tok[pi]], inc="sc")
            ps_g_tok[pi] = t_sg
            t_h = p.op(
                "vector",
                lambda eng, tc=tc, pi=pi: eng.scalar_tensor_tensor(
                    hbl[pi][:], ps_u[pi][:], negA_u[:, tc:tc + 1], sgb[pi][:],
                    ALU.mult, ALU.mult),
                waits=[mu_last, t_sg, t_nA[1], hbl_tok[pi]], inc="v")
            ps_u_tok[pi] = t_h
            sgb_tok[pi] = t_h
            t_hs = p.dma("sync", "hsa" if pi == 0 else "hsb",
                         h_dst[:, tc, ot * 512:(ot + 1) * 512],
                         hbl[pi][:], waits=[t_h])
            hbl_tok[pi] = t_hs
            t_h_last = t_hs
            if stages and tc % 2 == 1:
                stages.pop(0)()
        for s in stages:
            s()
        wgq_tok[oi] = mg_last
        wuq_tok[oi] = mu_last
    t_f1_vfence = p.tok("v")
    t_hs_tot = [("hsa", p.counts["hsa"]), ("hsb", p.counts["hsb"])]
    f1.close()

    p.phase_barrier()

    t_f1_scfence = p.tok("sc")
    f3 = ExitStack()
    f3sb = lambda name, shape, dtype=F32: f3.enter_context(
        nc.sbuf_tensor(name, shape, dtype, side="right"))
    h_qT = f3sb("h_qT", [128, NJC, 512], BF16)
    hrow = [f3sb("hrow0", [128, I])]
    hq_tm = [f3sb(f"hq_tm{i}", [128, I], BF16) for i in range(2)]
    rr3 = f3sb("rr3", [128, I // 2])
    hstat = f3sb("hstat", [128, 2])
    hs_row = f3sb("hs_row", [1, 512])
    hsb_sb = f3sb("hsb_sb", [128, 512])
    wdb = [f3sb(f"wdb{i}", [128, 8, 1024], BF16) for i in range(2)]
    yblk = [f3sb(f"yblk{i}", [128, 512]) for i in range(2)]
    pfb = [f3sb(f"pfb{i}", [128, 512]) for i in range(2)]
    ps_y = [f3.enter_context(nc.psum_tensor(f"ps_y{i}", [128, 512], F32))
            for i in range(8)]

    h_src = h_sp.rearrange("(a p) o -> p a o", p=128)
    wdq_src4 = wdq_sp.rearrange("(a p) c -> p a c", p=128)
    hrow_tok = [None]
    hq_tok = [None, None]
    t_hr_prev = []
    t_dn_vlast = t_f1_vfence
    yblk_tok = [None, None]
    pfb_tok = [None, None]
    wdb_tok = [None, None]

    def emit_hquant(tt, gate):
        toks_hs = []
        toks_hx = []
        for sub in range(4):
            tc = tt * 4 + sub
            par = sub % 2
            tl = p.dma("gpsimd", "hrl", hrow[0][:], h_src[:, tc, :],
                       waits=[hrow_tok[0]] +
                             (t_hs_tot if (tt == 0 and sub == 0) else []))
            p.op("vector",
                 lambda eng: eng.reduce_max(
                     hstat[:, 0:1], hrow[0][:], axis=mybir.AxisListType.X,
                     apply_absolute_value=True),
                 waits=[tl, t_f1_scfence], inc="v")
            vpad()
            t_hsv = p.op("vector",
                         lambda eng, sub=sub: eng.tensor_scalar(
                             hs_all[:, sub:sub + 1], hstat[:, 0:1],
                             1.0 / 127.0, 1e-5, ALU.mult, ALU.add),
                         waits=list(t_hr_prev), inc="v")
            toks_hs.append(t_hsv)
            vpad()
            t_rhs = p.op("vector",
                         lambda eng, sub=sub: eng.reciprocal(
                             hstat[:, 1:2], hs_all[:, sub:sub + 1]),
                         inc="v")
            t_hq = None
            for hv in range(2):
                t_rr = p.op("scalar",
                            lambda eng, hv=hv: eng.activation(
                                rr3[:], hrow[0][:, hv * 4096:(hv + 1) * 4096],
                                AF.Copy, bias=C2P23, scale=hstat[:, 1:2]),
                            waits=[t_rhs, t_conj, t_hq], inc="sc")
                t_hq = p.op("vector",
                            lambda eng, hv=hv, par=par: eng.tensor_scalar(
                                hq_tm[par][:, hv * 4096:(hv + 1) * 4096],
                                rr3[:], C2P23, None, ALU.subtract),
                            waits=[t_rr, hq_tok[par]], inc="v")
                hrow_tok[0] = t_rr
            t_hx = p.dma("sync", "hqx",
                         h_qT[:, :, sub * 128:(sub + 1) * 128],
                         hq_tm[par].ap().rearrange("p (a b) -> p a b", b=128),
                         waits=[t_hq] + list(gate), transpose=True)
            hq_tok[par] = t_hx
            toks_hx.append(t_hx)
        return toks_hs, toks_hx

    t_out_toks = []
    hq_state = emit_hquant(0, [])
    for tt in range(4):
        hs_toks, hx_toks = hq_state
        t_hr = []
        for sub in range(4):
            t_hr.append(p.op(
                "tensor",
                lambda eng, sub=sub: eng.matmul(
                    ps_y[0][0:1, sub * 128:(sub + 1) * 128],
                    hs_all[:, sub:sub + 1], ident[:], start=True, stop=True),
                waits=[hs_toks[sub], t_ident, t_dn_vlast, t_f1_scfence],
                inc="te"))
        t_hr_prev[:] = t_hr
        t_hrow = p.op("vector",
                      lambda eng: eng.tensor_scalar(
                          hs_row[0:1, :], ps_y[0][0:1, :], ws_vec[0:1, 2:3],
                          None, ALU.mult),
                      waits=t_hr + [t_wsb], inc="v")
        t_hbc = p.op("tensor",
                     lambda eng: eng.matmul(ps_y[1][:], ones_row[0:1, :],
                                            hs_row[0:1, :], start=True,
                                            stop=True),
                     waits=[t_hrow, t_conj], inc="te")
        t_hsb = p.op("vector",
                     lambda eng: eng.tensor_copy(hsb_sb[:], ps_y[1][:]),
                     waits=[t_hbc], inc="v")
        if os.environ.get("KDBG") and tt == 0:
            dh1 = p.dma("sync", "dbq", dbg_hqT,
                        h_qT.ap().rearrange("p a b -> p (a b)"),
                        waits=[hx_toks[-1]])
            dh2 = p.dma("sync", "dbq", dbg_hsb, hsb_sb[:], waits=[t_hsb])
            for e in ENGINES:
                p.op(e, None, waits=[dh1, dh2])
        for dh in range(2):
            mm_last = [None] * 8
            for jg in range(NJC // 8):
                wi = jg % 2
                twd = p.dma("gpsimd", "wda" if wi == 0 else "wdb2",
                            wdb[wi][:],
                            wdq_src4[:, jg * 8:(jg + 1) * 8,
                                     dh * 1024:(dh + 1) * 1024],
                            waits=[wdb_tok[wi], t_wdq_last])
                for j8 in range(8):
                    jc = jg * 8 + j8
                    for i8 in range(8):
                        mm_last[i8] = p.op(
                            "tensor",
                            lambda eng, jc=jc, j8=j8, i8=i8, wi=wi: eng.matmul(
                                ps_y[i8][:],
                                wdb[wi][:, j8, i8 * 128:(i8 + 1) * 128],
                                h_qT[:, jc, :],
                                start=(jc == 0), stop=(jc == NJC - 1)),
                            waits=([twd, t_hsb, t_dn_vlast,
                                    hx_toks[-1] if jc == 0 else None]
                                   if (i8 == 0 and j8 == 0) else ()),
                            inc="te")
                wdb_tok[wi] = mm_last[7]
            if dh == 1 and tt + 1 < 4:
                hq_state = emit_hquant(tt + 1, [mm_last[7]])
            for i8 in range(8):
                dc = dh * 8 + i8
                bi2 = i8 % 2
                tpf = p.dma("sync", "pfa" if bi2 == 0 else "pfb2",
                            pfb[bi2][:],
                            part_sp[dc * 128:(dc + 1) * 128,
                                    tt * 512:(tt + 1) * 512],
                            waits=[pfb_tok[bi2], t_part[dc][tt]])
                t_y1 = p.op(
                    "vector",
                    lambda eng, i8=i8, bi2=bi2: eng.tensor_tensor(
                        yblk[bi2][:], ps_y[i8][:], hsb_sb[:], ALU.mult),
                    waits=[mm_last[i8], t_hsb, yblk_tok[bi2]], inc="v")
                t_y2 = p.op(
                    "vector",
                    lambda eng, bi2=bi2: eng.tensor_tensor(
                        yblk[bi2][:], pfb[bi2][:], yblk[bi2][:],
                        ALU.subtract),
                    waits=[tpf], inc="v")
                pfb_tok[bi2] = t_y2
                t_o = p.dma("sync", "yoa" if bi2 == 0 else "yob",
                            y_out[dc * 128:(dc + 1) * 128,
                                  tt * 512:(tt + 1) * 512],
                            yblk[bi2][:], waits=[t_y2])
                yblk_tok[bi2] = t_o
                t_out_toks.append(t_o)
            t_dn_vlast = p.tok("v")
    f3.close()

    fin = [(s, p.counts[s]) for s in DMA_SEMS] + [
        ("v", p.counts["v"]), ("sc", p.counts["sc"]), ("te", p.counts["te"])]
    p.op("sync", None, waits=fin)
    p.op("gpsimd", None, waits=[(s, p.counts[s]) for s in DMA_SEMS])

    nums = sorted(h.num for h in p.sem.values())
    lo, hi = nums[0], nums[-1]
    nc.all_engine_barrier()
    nc.gpsimd.dma_reset(range(lo, hi + 1))
    nc.gpsimd.sem_clear(range(lo, hi + 1))
    nc.all_engine_barrier()

    with nc.Block() as block:
        p.emit(block)
    gstack.close()
    return nc


_ncache = {}


def _get_program():
    if "nc" not in _ncache:
        _ncache["nc"] = build_program()
    return _ncache["nc"]


def _rope_tables():
    e = (np.arange(0, HD, 2, dtype=np.float32) / np.float32(HD)).astype(
        np.float32)
    inv = (np.float32(1.0) / np.power(np.float32(10000.0), e)).astype(
        np.float32)
    ang = (np.arange(S, dtype=np.float32)[:, None] * inv[None, :]).astype(
        np.float32)
    return np.cos(ang).astype(np.float32), np.sin(ang).astype(np.float32)


def kernel(L0_attn_x, L0_attn_ln_w, L0_attn_wq, L0_attn_wk, L0_attn_wv,
           L0_attn_wo, L0_ffn_w_gate, L0_ffn_w_up, L0_ffn_w_down):
    x = np.ascontiguousarray(np.asarray(L0_attn_x, dtype=np.float32))
    lnw = np.ascontiguousarray(
        np.asarray(L0_attn_ln_w, dtype=np.float32).reshape(NIC, 128))
    permH = np.concatenate([np.arange(0, 128, 2), np.arange(1, 128, 2)])
    permq = np.concatenate([hh * 128 + permH for hh in range(H)])
    permk = np.concatenate([hh * 128 + permH for hh in range(KH)])
    wq = np.ascontiguousarray(np.asarray(L0_attn_wq, np.float32)[:, permq])
    wk = np.ascontiguousarray(np.asarray(L0_attn_wk, np.float32)[:, permk])
    wv = np.ascontiguousarray(np.asarray(L0_attn_wv, np.float32))
    wo = np.ascontiguousarray(np.asarray(L0_attn_wo, np.float32))
    wgT = np.ascontiguousarray(np.asarray(L0_ffn_w_gate, np.float32).T)
    wuT = np.ascontiguousarray(np.asarray(L0_ffn_w_up, np.float32).T)
    wdT = np.ascontiguousarray(np.asarray(L0_ffn_w_down, np.float32).T)
    pp = np.arange(128)[:, None]
    cc = np.arange(640)[None, :]
    mask = np.where((cc >= pp) & (cc <= pp + W), 0.0, NEG).astype(np.float32)
    ropeC, ropeS = _rope_tables()
    ident = np.eye(128, dtype=np.float32)

    nc = _get_program()
    common = dict(lnw=lnw, wq=wq, wk=wk, wv=wv, wo=wo, wgT=wgT, wuT=wuT,
                  wdT=wdT, mask=mask, ropeC=ropeC, ropeS=ropeS, ident=ident)
    in_maps = []
    for b in range(8):
        m = dict(common)
        m["x"] = np.ascontiguousarray(x[b])
        m["xT"] = np.ascontiguousarray(x[b].T)
        in_maps.append(m)
    res = run_bass_kernel_spmd(nc, in_maps, core_ids=list(range(8)))
    out = np.stack([np.asarray(r["y"], np.float32).T for r in res.results],
                   axis=0)
    return out
